# version 01
# baseline (speedup 1.0000x reference)
import math
import numpy as np
import concourse.bass as bass
import concourse.mybir as mybir
from concourse.bass_utils import run_bass_kernel_spmd

F32 = mybir.dt.float32
BF16 = mybir.dt.bfloat16
ALU = mybir.AluOpType
AX = mybir.AxisListType
AF = mybir.ActivationFunctionType

D = 1024
T = 4096
NT = 32
EPS = 1e-5

SEMS = [
    "dX", "dQ", "dK", "dV", "dW", "dSp", "dO",
    "gI",
    "vX", "vF", "sE", "tTr", "tQK",
    "tSC", "sSC", "vCM", "vB", "sEX", "vRD", "tTW", "vWT", "tPV", "vPS",
    "sRS", "vST", "vZ",
    "tG", "sG", "vMK", "vXS", "tXF", "vXF", "tGU", "sSI", "vAC", "tDN", "vYA",
    "tQKV", "sQE", "vEL", "tAT", "vMA", "tNU", "sNU", "vDV", "tKT", "vKT",
    "tSU", "vSU", "tOT", "vOT", "tWO",
    "tH", "sH", "vH", "dWH", "dC", "dCS",
]


class _Null:
    def __getattr__(self, k):
        def f(*a, **kw):
            return None
        return f


_NULL = _Null()


class Ctx:
    def __init__(self, nc, sems, eng_name, eng):
        self.nc = nc
        self.sems = sems
        self.e = eng_name
        self.eng = eng
        self.cnt = {k: 0 for k in sems}
        self.v = eng if eng_name == "vector" else _NULL
        self.s = eng if eng_name == "scalar" else _NULL
        self.te = eng if eng_name == "tensor" else _NULL
        self.g = eng if eng_name == "gpsimd" else _NULL
        self.sy = eng if eng_name == "sync" else _NULL

    def wait(self, engname, sem, val=None):
        if val is None:
            val = self.cnt[sem]
        if self.e == engname:
            self.eng.wait_ge(self.sems[sem], val)

    def inc(self, ins, sem, k=1):
        self.cnt[sem] += k
        if ins is not None:
            ins.then_inc(self.sems[sem], k)
        return self.cnt[sem]

    def val(self, sem):
        return self.cnt[sem]


def _carve(arena, byte_off, shape, dtype):
    nelem = int(np.prod(shape))
    esz = 4 if dtype == F32 else 2
    assert byte_off % 4 == 0
    start2 = byte_off // 2
    ap = arena[:, start2 : start2 + nelem * esz // 2]
    if dtype == F32:
        ap = ap.bitcast(F32)
    if len(shape) == 1:
        return ap
    spec = "p (" + " ".join(f"a{i}" for i in range(len(shape))) + ") -> p " + " ".join(
        f"a{i}" for i in range(len(shape))
    )
    return ap.rearrange(spec, **{f"a{i}": s for i, s in enumerate(shape)})


class Arena:
    def __init__(self, arena_ap, nbytes, off=0):
        self.ap = arena_ap
        self.nbytes = nbytes
        self.off = off

    def alloc(self, shape, dtype):
        esz = 4 if dtype == F32 else 2
        nelem = int(np.prod(shape))
        sz = nelem * esz
        off = (self.off + 3) // 4 * 4
        assert off + sz <= self.nbytes, f"arena overflow: {off}+{sz} > {self.nbytes}"
        v = _carve(self.ap, off, shape, dtype)
        self.off = off + sz
        return v


TILE_LAYOUT = "(s p) f -> p s f"


def emit_program(c, t, upto=4):
    emit_init(c, t)
    emit_p0(c, t)
    emit_p1(c, t)
    if upto >= 2:
        emit_p2(c, t)
    if upto >= 3:
        emit_p3(c, t)
    if upto >= 4:
        emit_p4(c, t)
    else:
        emit_stream_out(c, t)
    c.wait("scalar", "dO")


def emit_init(c, t):
    c.g.memset(t["ident"][:], 0.0)
    ins = c.g.affine_select(
        out=t["ident"][:], in_=t["ident"][:], compare_op=ALU.not_equal,
        fill=1.0, base=0, pattern=[[-1, 128]], channel_multiplier=1)
    c.inc(ins, "gI")



def emit_p0(c, t):
    a = Arena(t["arena"], t["arena_bytes"])
    wq = a.alloc([8, 1024], BF16)
    wk = a.alloc([8, 1024], BF16)
    wv = a.alloc([8, 1024], BF16)
    xt = a.alloc([2, 4, 1024], F32)
    xb = a.alloc([4, 1024], BF16)
    xfm = a.alloc([8, 512], BF16)
    qst = a.alloc([8, 512], BF16)
    kst = a.alloc([8, 512], BF16)
    vst = a.alloc([4, 1024], BF16)
    assert a.off <= t["arena_bytes"], a.off

    x = t["x"]
    pb = t["pb"]
    pbt = [pb[i].bitcast(BF16).rearrange("p (a b) -> p a b", a=8) for i in range(2)]

    for w_ap, dst in ((t["wqT"], wq), (t["wkT"], wk), (t["wvT"], wv)):
        src = w_ap.rearrange("(fc p) o -> p fc o", p=128)
        c.inc(c.g.dma_start(out=dst[:], in_=src), "dW", 16)

    xb_done = []
    tr_done = []
    st_done = []
    mm_groups = [0]

    for m in range(8):
        buf = m % 2
        if m >= 2:
            c.wait("sync", "vX", xb_done[m - 2])
        ins = c.sy.dma_start(out=xt[:, buf], in_=x[m * 512 : (m + 1) * 512, :].rearrange(TILE_LAYOUT, p=128))
        c.inc(ins, "dX", 16)

        c.wait("vector", "dX", 16 * (m + 1))
        if m >= 1:
            c.wait("vector", "tTr", tr_done[m - 1])
        ins = c.v.tensor_copy(xb[:], xt[:, buf])
        xb_done.append(c.inc(ins, "vX"))

        c.wait("tensor", "vX", xb_done[m])
        if m == 0:
            c.wait("tensor", "gI", 1)
        for g in range(4):
            bk = g % 2
            if m * 4 + g >= 2:
                c.wait("tensor", "vF", c.val("vF") - 1)
            ins = None
            for fc in range(8):
                ins = c.te.transpose(pbt[bk][:, fc, :], xb[:, g, fc * 128 : (fc + 1) * 128], t["ident"][:])
            c.inc(ins, "tTr")
            c.wait("vector", "tTr", c.val("tTr"))
            ins = c.v.tensor_copy(xfm[:, :, g * 128 : (g + 1) * 128], pbt[bk][:, :, :])
            c.inc(ins, "vF")
        tr_done.append(c.val("tTr"))

        c.wait("tensor", "vF", c.val("vF"))
        if m == 0:
            c.wait("tensor", "dW", 48)
        if m >= 1:
            c.wait("tensor", "dSp", st_done[m - 1])

        def mmgroup(emit_mms, evac_dst):
            i = mm_groups[0]
            mm_groups[0] += 1
            bank = 2 + (i % 4)
            if i >= 4:
                c.wait("tensor", "sE", i - 3)
            ins = emit_mms(pb[bank])
            c.inc(ins, "tQK")
            c.wait("scalar", "tQK", c.val("tQK"))
            ins = c.s.copy(evac_dst, pb[bank])
            c.inc(ins, "sE")

        for wsb, stage in ((wq, qst), (wk, kst)):
            for oc in range(8):
                def mk(ps, wsb=wsb, oc=oc):
                    ins = None
                    for fc in range(8):
                        ins = c.te.matmul(ps, wsb[:, fc, oc * 128 : (oc + 1) * 128], xfm[:, fc, :],
                                          start=(fc == 0), stop=(fc == 7))
                    return ins
                mmgroup(mk, stage[:, oc, :])
        for s in range(4):
            for half in range(2):
                def mk(ps, s=s, half=half):
                    ins = None
                    for fc in range(8):
                        ins = c.te.matmul(ps, xfm[:, fc, s * 128 : (s + 1) * 128],
                                          wv[:, fc, half * 512 : (half + 1) * 512],
                                          start=(fc == 0), stop=(fc == 7))
                    return ins
                mmgroup(mk, vst[:, s, half * 512 : (half + 1) * 512])

        c.s.drain()
        c.inc(c.s.dma_start(out=t["q_spill"][m], in_=qst[:]), "dSp", 16)
        c.inc(c.s.dma_start(out=t["k_spill"][m], in_=kst[:]), "dSp", 16)
        c.inc(c.s.dma_start(out=t["v_spill"][m], in_=vst[:]), "dSp", 16)
        st_done.append(c.val("dSp"))



def emit_p1(c, t):
    a = Arena(t["arena"], t["arena_bytes"])
    qbuf = a.alloc([8, 512], BF16)
    kch = a.alloc([2, 8, 512], BF16)
    vch = a.alloc([2, 4, 1024], BF16)
    scb = a.alloc([4, 8, 512], BF16)
    wT = a.alloc([32, 512], BF16)
    xres = a.alloc([4, 1024], F32)
    z2 = a.alloc([2, 1024], F32)
    zsq = a.alloc([1024], F32)
    cmax = a.alloc([4, 8], F32)
    st8 = a.alloc([12, 4], F32)
    assert a.off <= t["arena_bytes"], a.off

    rmneg = st8[:, 0, :]
    bias4 = st8[:, 1, :]
    den4 = st8[:, 2, :]
    rden4 = st8[:, 3, :]
    s1c = st8[:, 4, :]
    s2c = st8[:, 5, :]
    mean4 = st8[:, 6, :]
    msq4 = st8[:, 7, :]
    var4 = st8[:, 8, :]
    lnv4 = st8[:, 9, :]
    rstd4 = st8[:, 10, :]

    pb = t["pb"]
    pbt = [pb[i].bitcast(BF16).rearrange("p (a b) -> p a b", a=8) for i in range(2)]
    x = t["x"]

    p0_dsp = c.val("dSp")
    p0_dx = c.val("dX")
    qk_used = {}
    kch_used = {}
    vch_used = {}
    epi_done = []
    ps_rel = []

    for m in range(8):
        if m == 0:
            c.wait("sync", "dSp", p0_dsp)
        if m >= 1:
            c.wait("sync", "tSC", qk_used[m - 1])
        ins = c.sy.dma_start(out=qbuf[:], in_=t["q_spill"][m])
        c.inc(ins, "dQ", 16)
        if m >= 1:
            c.wait("sync", "vZ", epi_done[m - 1])
        ins = c.sy.dma_start(out=xres[:], in_=x[m * 512 : (m + 1) * 512, :].rearrange(TILE_LAYOUT, p=128))
        c.inc(ins, "dX", 16)

        if m >= 1:
            c.wait("tensor", "vPS", ps_rel[m - 1])
        for kc in range(8):
            kb = kc % 2
            gidx = m * 8 + kc
            if gidx >= 2:
                c.wait("sync", "tSC", kch_used[gidx - 2])
            ins = c.sy.dma_start(out=kch[:, kb], in_=t["k_spill"][kc])
            c.inc(ins, "dK", 16)
            c.wait("tensor", "dK", 16 * (gidx + 1))
            if kc == 0:
                c.wait("tensor", "dQ", 16 * (m + 1))
            for s in range(4):
                bank = (0 if kc % 2 == 0 else 4) + s
                if gidx >= 2:
                    c.wait("tensor", "sSC", c.val("sSC") - 7)
                ins = None
                for fc in range(8):
                    ins = c.te.matmul(pb[bank], qbuf[:, fc, s * 128 : (s + 1) * 128],
                                      kch[:, kb, fc, :], start=(fc == 0), stop=(fc == 7))
                c.inc(ins, "tSC")
                c.wait("scalar", "tSC", c.val("tSC"))
                ins = c.s.copy(scb[:, s, kc, :], pb[bank])
                c.inc(ins, "sSC")
                c.wait("vector", "sSC", c.val("sSC"))
                ins = c.v.reduce_max(cmax[:, s, kc : kc + 1], scb[:, s, kc, :], axis=AX.X)
                c.inc(ins, "vCM")
            kch_used[gidx] = c.val("tSC")
        qk_used[m] = c.val("tSC")

        c.wait("vector", "vCM", c.val("vCM"))
        for s in range(4):
            c.v.reduce_max(rmneg[:, s : s + 1], cmax[:, s, :], axis=AX.X, negate=True)
        c.v.drain()
        c.v.tensor_scalar(bias4[:], rmneg[:], 1.0 / 32.0, None, ALU.mult)
        c.v.drain()
        c.inc(c.v.nop(), "vB")

        c.wait("scalar", "vB", c.val("vB"))
        for s in range(4):
            ins = c.s.activation(scb[:, s].rearrange("p a b -> p (a b)"),
                                 scb[:, s].rearrange("p a b -> p (a b)"),
                                 AF.Exp, bias=bias4[:, s : s + 1], scale=1.0 / 32.0,
                                 accum_out=den4[:, s : s + 1])
            c.inc(ins, "sEX")

        c.wait("vector", "sEX", c.val("sEX"))
        c.v.reciprocal(rden4[:], den4[:])
        c.v.drain()
        c.inc(c.v.nop(), "vRD")

        c.wait("tensor", "sEX", c.val("sEX"))
        for s in range(4):
            for kc in range(8):
                g = s * 8 + kc
                bk = g % 2
                if g >= 2:
                    c.wait("tensor", "vWT", c.val("vWT") - 1)
                ins = None
                for j in range(4):
                    ins = c.te.transpose(pbt[bk][:, j, :], scb[:, s, kc, j * 128 : (j + 1) * 128], t["ident"][:])
                c.inc(ins, "tTW")
                c.wait("vector", "tTW", c.val("tTW"))
                ins = c.v.tensor_copy(wT[:, kc * 4 : kc * 4 + 4, s * 128 : (s + 1) * 128], pbt[bk][:, 0:4, :])
                c.inc(ins, "vWT")

        c.wait("tensor", "vWT", c.val("vWT"))
        for vc in range(8):
            vb = vc % 2
            gidx = m * 8 + vc
            if gidx >= 2:
                c.wait("sync", "tPV", vch_used[gidx - 2])
            ins = c.sy.dma_start(out=vch[:, vb], in_=t["v_spill"][vc])
            c.inc(ins, "dV", 16)
            c.wait("tensor", "dV", 16 * (gidx + 1))
            ins = None
            for s in range(4):
                for j in range(4):
                    kcg = vc * 4 + j
                    for half in range(2):
                        ins = c.te.matmul(pb[2 * s + half],
                                          wT[:, kcg, s * 128 : (s + 1) * 128],
                                          vch[:, vb, j, half * 512 : (half + 1) * 512],
                                          start=(kcg == 0), stop=(kcg == 31))
            c.inc(ins, "tPV")
            vch_used[gidx] = c.val("tPV")

        c.wait("vector", "dX", p0_dx + 16 * (m + 1))
        c.wait("vector", "vRD", c.val("vRD"))
        c.wait("vector", "tPV", c.val("tPV"))
        for pair in range(2):
            for k in range(2):
                s = pair * 2 + k
                psflat = t["pb_flat"][:, (2 * s) * 512 : (2 * s + 2) * 512]
                ins = c.v.scalar_tensor_tensor(z2[:, k, :], psflat, rden4[:, s : s + 1],
                                               xres[:, s, :], ALU.mult, ALU.add)
                c.inc(ins, "vPS")
                c.v.drain()
                c.v.reduce_sum(s1c[:, s : s + 1], z2[:, k, :], axis=AX.X)
                c.v.tensor_tensor(zsq[:], z2[:, k, :], z2[:, k, :], ALU.mult)
                c.v.drain()
                c.v.reduce_sum(s2c[:, s : s + 1], zsq[:], axis=AX.X)
            c.v.drain()
            sl = slice(pair * 2, pair * 2 + 2)
            c.v.tensor_scalar(mean4[:, sl], s1c[:, sl], 1.0 / 1024.0, None, ALU.mult)
            c.v.drain()
            c.v.tensor_tensor(msq4[:, sl], mean4[:, sl], mean4[:, sl], ALU.mult)
            c.v.drain()
            c.v.scalar_tensor_tensor(var4[:, sl], s2c[:, sl], 1.0 / 1024.0, msq4[:, sl],
                                     ALU.mult, ALU.subtract)
            c.v.drain()
            c.v.tensor_scalar(var4[:, sl], var4[:, sl], EPS, None, ALU.add)
            c.v.drain()
            c.inc(c.v.nop(), "vST")
            c.wait("scalar", "vST", c.val("vST"))
            c.s.activation(lnv4[:, sl], var4[:, sl], AF.Ln)
            c.s.drain()
            c.s.activation(rstd4[:, sl], lnv4[:, sl], AF.Exp, scale=-0.5)
            c.s.drain()
            c.inc(c.s.nop(), "sRS")
            c.wait("vector", "sRS", c.val("sRS"))
            for k in range(2):
                s = pair * 2 + k
                ins = c.v.tensor_scalar(t["stream"][:, m * 4 + s, :], z2[:, k, :],
                                        mean4[:, s : s + 1], rstd4[:, s : s + 1],
                                        ALU.subtract, ALU.mult)
                c.inc(ins, "vZ")
        epi_done.append(c.val("vZ"))
        ps_rel.append(c.val("vPS"))


def emit_stream_out(c, t):
    a = Arena(t["arena"], t["arena_bytes"])
    stg = a.alloc([2, 1024], F32)
    do0 = c.val("dO")
    for ti in range(NT):
        b = ti % 2
        if ti >= 2:
            c.wait("vector", "dO", do0 + 16 * (ti - 1))
        c.wait("vector", "vZ", c.val("vZ"))
        ins = c.v.tensor_copy(stg[:, b, :], t["stream"][:, ti, :])
        c.inc(ins, "vZ")
        c.wait("scalar", "vZ", c.val("vZ"))
        ins = c.s.dma_start(out=t["out"][ti * 128 : (ti + 1) * 128, :], in_=stg[:, b, :])
        c.inc(ins, "dO", 16)



def emit_p2(c, t):
    a = Arena(t["arena"], t["arena_bytes"])
    x_fm = a.alloc([8, 1024], BF16)
    xs_fm = a.alloc([8, 1024], BF16)
    y_acc = a.alloc([8, 1024], F32)
    gate_p = a.alloc([2, 8, 512], BF16)
    up_p = a.alloc([2, 8, 512], BF16)
    down_p = a.alloc([2, 4, 1024], BF16)
    act_q = a.alloc([2, 4, 512], BF16)
    sg = a.alloc([512], F32)
    gw = a.alloc([8, 8], BF16)
    lg8 = a.alloc([8, 8], F32)
    rm8 = a.alloc([8], F32)
    sc8 = a.alloc([8], F32)
    masks8 = a.alloc([8, 8], F32)
    xst = a.alloc([1024], BF16)
    zsq = a.alloc([1024], F32)
    st16 = a.alloc([8, 8], F32)
    assert a.off <= t["arena_bytes"], a.off

    s1c = st16[:, 0, :]
    s2c = st16[:, 1, :]
    mean8 = st16[:, 2, :]
    msq8 = st16[:, 3, :]
    var8 = st16[:, 4, :]
    lnv8 = st16[:, 5, :]
    rstd8 = st16[:, 6, :]

    pb = t["pb"]
    pbt = [pb[2 + i].bitcast(BF16).rearrange("p (a b) -> p a b", a=8) for i in range(2)]

    p1_tpv = c.val("tPV")
    p1_vz = c.val("vZ")

    c.wait("gpsimd", "tPV", p1_tpv)
    c.wait("gpsimd", "vZ", p1_vz)
    ins = c.g.dma_start(out=gw[:], in_=t["gate_w"].rearrange("(fc p) e -> p fc e", p=128))
    c.inc(ins, "dW", 16)
    gw_dw = c.val("dW")

    piece_used = []

    for qq in range(4):
        tiles = [qq * 8 + su for su in range(8)]
        c.wait("tensor", "vZ", p1_vz)

        import os as _os
        for su in range(8):
            ti = tiles[su]
            gidx = qq * 16 + su
            bk = gidx % 2
            if qq + su > 0:
                c.wait("tensor", "vXF", c.val("vXF"))
            ins = None
            for fc in range(8):
                ins = c.te.transpose(pbt[bk][:, fc, :], t["stream"][:, ti, fc * 128 : (fc + 1) * 128], t["ident"][:])
            c.inc(ins, "tXF")
            c.wait("vector", "tXF", c.val("tXF"))
            c.v.nop(cycle_cnt=2000)
            ins = c.v.tensor_copy(x_fm[:, :, su * 128 : (su + 1) * 128], pbt[bk][:, :, :])
            c.inc(ins, "vXF")
            gbank = 6 + (su % 2)
            c.wait("tensor", "vXF", c.val("vXF"))
            if su >= 2:
                c.wait("tensor", "sG", c.val("sG") - 1)
            if qq == 0 and su == 0:
                c.wait("tensor", "dW", gw_dw)
            ins = None
            for fc in range(8):
                ins = c.te.matmul(pb[gbank][:, 0:8], x_fm[:, fc, su * 128 : (su + 1) * 128], gw[:, fc, :],
                                  start=(fc == 0), stop=(fc == 7))
            c.inc(ins, "tG")
            c.wait("scalar", "tG", c.val("tG"))
            ins = c.s.copy(lg8[:, su, :], pb[gbank][:, 0:8])
            c.inc(ins, "sG")
        c.wait("vector", "sG", c.val("sG"))
        for su in range(8):
            c.v.reduce_max(rm8[:, su : su + 1], lg8[:, su, :], axis=AX.X)
        c.v.drain()
        for su in range(8):
            c.v.tensor_scalar(masks8[:, su, :], lg8[:, su, :], rm8[:, su : su + 1], None, ALU.is_equal)
        c.v.drain()
        c.inc(c.v.nop(), "vMK")
        c.wait("scalar", "vMK", c.val("vMK"))
        ins = c.s.activation(sc8[:], rm8[:], AF.Sigmoid)
        c.s.drain()
        c.inc(c.s.nop(), "sG")

        c.wait("vector", "sG", c.val("sG"))
        for su in range(8):
            ti = tiles[su]
            ins = c.v.tensor_scalar(xst[:], t["stream"][:, ti, :], sc8[:, su : su + 1], None, ALU.mult)
            c.inc(ins, "vXS")
            gidx = su
            bk = su % 2
            c.wait("tensor", "vXS", c.val("vXS"))
            if su >= 1:
                c.wait("tensor", "vXF", c.val("vXF"))
            ins = None
            for fc in range(8):
                ins = c.te.transpose(pbt[bk][:, fc, :], xst[:, fc * 128 : (fc + 1) * 128], t["ident"][:])
            c.inc(ins, "tXF")
            c.wait("vector", "tXF", c.val("tXF"))
            ins = c.v.tensor_copy(xs_fm[:, :, su * 128 : (su + 1) * 128], pbt[bk][:, :, :])
            c.inc(ins, "vXF")

        c.v.memset(y_acc[:], 0.0)

        for e in range(9):
            for iq in range(4):
                pidx = len(piece_used)
                pbuf = pidx % 2
                if pidx >= 2:
                    c.wait("gpsimd", "tDN", piece_used[pidx - 2])
                if e < 8:
                    gsrc = t["rt_gate"][e, :, iq * 512 : (iq + 1) * 512]
                    usrc = t["rt_up"][e, :, iq * 512 : (iq + 1) * 512]
                    dsrc = t["rt_down"][e, iq * 512 : (iq + 1) * 512, :]
                else:
                    gsrc = t["sh_gate"][:, iq * 512 : (iq + 1) * 512]
                    usrc = t["sh_up"][:, iq * 512 : (iq + 1) * 512]
                    dsrc = t["sh_down"][iq * 512 : (iq + 1) * 512, :]
                c.inc(c.g.dma_start(out=gate_p[:, pbuf], in_=gsrc.rearrange("(fc p) i -> p fc i", p=128)), "dW", 16)
                c.inc(c.g.dma_start(out=up_p[:, pbuf], in_=usrc.rearrange("(fc p) i -> p fc i", p=128)), "dW", 16)
                c.inc(c.g.dma_start(out=down_p[:, pbuf], in_=dsrc.rearrange("(ic p) h -> p ic h", p=128)), "dW", 16)
                dw_here = c.val("dW")
                xin = x_fm if e == 8 else xs_fm
                for tt in range(2):
                    for ic in range(4):
                        gk = (ic % 2) * 2
                        uk = gk + 1
                        c.wait("tensor", "dW", dw_here - 16)
                        if True:
                            c.wait("tensor", "vAC", c.val("vAC") - 1 if (c.val("vAC") >= 2) else 0)
                        ins = None
                        for fc in range(8):
                            ins = c.te.matmul(pb[gk], gate_p[:, pbuf, fc, ic * 128 : (ic + 1) * 128],
                                              xin[:, fc, tt * 512 : (tt + 1) * 512],
                                              start=(fc == 0), stop=(fc == 7))
                        c.inc(ins, "tGU")
                        ins = None
                        for fc in range(8):
                            ins = c.te.matmul(pb[uk], up_p[:, pbuf, fc, ic * 128 : (ic + 1) * 128],
                                              xin[:, fc, tt * 512 : (tt + 1) * 512],
                                              start=(fc == 0), stop=(fc == 7))
                        c.inc(ins, "tGU")
                        c.wait("scalar", "tGU", c.val("tGU") - 1)
                        c.wait("scalar", "vAC", c.val("vAC"))
                        ins = c.s.activation(sg[:], pb[gk], AF.Silu)
                        c.inc(ins, "sSI")
                        c.wait("vector", "tGU", c.val("tGU"))
                        c.wait("vector", "sSI", c.val("sSI"))
                        ins = c.v.tensor_tensor(act_q[:, tt, ic, :], sg[:], pb[uk], ALU.mult)
                        c.inc(ins, "vAC")
                    for s4 in range(4):
                        su = tt * 4 + s4
                        dbank = 4 if (su % 2 == 0) else 6
                        c.wait("tensor", "dW", dw_here)
                        c.wait("tensor", "vAC", c.val("vAC"))
                        if c.val("tDN") >= 2:
                            c.wait("tensor", "vYA", c.val("vYA") - 1)
                        ins = None
                        for ic in range(4):
                            for nh in range(2):
                                ins = c.te.matmul(pb[dbank + nh], act_q[:, tt, ic, s4 * 128 : (s4 + 1) * 128],
                                                  down_p[:, pbuf, ic, nh * 512 : (nh + 1) * 512],
                                                  start=(ic == 0), stop=(ic == 3))
                        c.inc(ins, "tDN")
                        c.wait("vector", "tDN", c.val("tDN"))
                        psflat = t["pb_flat"][:, dbank * 512 : (dbank + 2) * 512]
                        if e < 8:
                            ins = c.v.scalar_tensor_tensor(y_acc[:, su, :], psflat, masks8[:, su, e : e + 1],
                                                           y_acc[:, su, :], ALU.mult, ALU.add)
                        else:
                            ins = c.v.scalar_tensor_tensor(y_acc[:, su, :], psflat, 1.0,
                                                           y_acc[:, su, :], ALU.mult, ALU.add)
                        c.inc(ins, "vYA")
                piece_used.append(c.val("tDN"))

        for su in range(8):
            ti = tiles[su]
            ins = c.v.tensor_tensor(y_acc[:, su, :], t["stream"][:, ti, :], y_acc[:, su, :], ALU.add)
        c.v.drain()
        for su in range(8):
            c.v.reduce_sum(s1c[:, su : su + 1], y_acc[:, su, :], axis=AX.X)
            c.v.tensor_tensor(zsq[:], y_acc[:, su, :], y_acc[:, su, :], ALU.mult)
            c.v.drain()
            c.v.reduce_sum(s2c[:, su : su + 1], zsq[:], axis=AX.X)
        c.v.drain()
        c.v.tensor_scalar(mean8[:], s1c[:], 1.0 / 1024.0, None, ALU.mult)
        c.v.drain()
        c.v.tensor_tensor(msq8[:], mean8[:], mean8[:], ALU.mult)
        c.v.drain()
        c.v.scalar_tensor_tensor(var8[:], s2c[:], 1.0 / 1024.0, msq8[:], ALU.mult, ALU.subtract)
        c.v.drain()
        c.v.tensor_scalar(var8[:], var8[:], EPS, None, ALU.add)
        c.v.drain()
        c.inc(c.v.nop(), "vST")
        c.wait("scalar", "vST", c.val("vST"))
        c.s.activation(lnv8[:], var8[:], AF.Ln)
        c.s.drain()
        c.s.activation(rstd8[:], lnv8[:], AF.Exp, scale=-0.5)
        c.s.drain()
        c.inc(c.s.nop(), "sRS")
        c.wait("vector", "sRS", c.val("sRS"))
        for su in range(8):
            ti = tiles[su]
            ins = c.v.tensor_scalar(t["stream"][:, ti, :], y_acc[:, su, :],
                                    mean8[:, su : su + 1], rstd8[:, su : su + 1],
                                    ALU.subtract, ALU.mult)
            c.inc(ins, "vZ")



def emit_p3(c, t):
    a = Arena(t["arena"], t["arena_bytes"])
    wqkv = a.alloc([8, 3072], BF16)
    wout = a.alloc([8, 1024], BF16)
    x_fm = a.alloc([8, 256], BF16)
    raw = a.alloc([1024], F32)
    q_tm = a.alloc([2, 1024], BF16)
    k_tm = a.alloc([2, 1024], BF16)
    q64 = a.alloc([16, 256], BF16)
    k64 = a.alloc([16, 256], BF16)
    v_ext = a.alloc([2, 16, 65], BF16)
    atb = a.alloc([2, 256], BF16)
    num_sb = a.alloc([2, 16, 65], F32)
    ov = a.alloc([2, 1024], BF16)
    o_fm = a.alloc([8, 256], BF16)
    s_sb = a.alloc([16, 65], F32)
    s_bf = a.alloc([16, 65], BF16)
    m0 = a.alloc([256], F32)
    m1 = a.alloc([256], F32)
    z1k = a.alloc([1024], F32)
    zsq = a.alloc([1024], F32)
    elu_t = zsq
    st4 = a.alloc([8, 2], F32)
    assert a.off <= t["arena_bytes"], a.off

    s1c = st4[:, 0, :]
    s2c = st4[:, 1, :]
    mean2 = st4[:, 2, :]
    msq2 = st4[:, 3, :]
    var2 = st4[:, 4, :]
    lnv2 = st4[:, 5, :]
    rstd2 = st4[:, 6, :]

    pb = t["pb"]
    pbt = [pb[i].bitcast(BF16).rearrange("p (a b) -> p a b", a=8) for i in range(2)]
    pp = [t["pb_flat"][:, 2 * 512 : 4 * 512], t["pb_flat"][:, 4 * 512 : 6 * 512]]
    p6 = pb[6].rearrange("p (a b) -> p a b", a=2)
    p7 = pb[7][:, 0 : 130].rearrange("p (a b) -> p a b", a=2)

    p2_vz = c.val("vZ")

    c.wait("gpsimd", "vZ", p2_vz)
    c.inc(c.g.dma_start(out=wqkv[:], in_=t["wqkvT"].rearrange("(fc p) o -> p fc o", p=128)), "dW", 16)
    c.inc(c.g.dma_start(out=wout[:], in_=t["woutT"].rearrange("(fc p) o -> p fc o", p=128)), "dW", 16)
    dw_p3 = c.val("dW")
    c.g.memset(m0[:], 1.0)
    ins = c.g.affine_select(out=m0[:], in_=m0[:], compare_op=ALU.is_ge, fill=0.0,
                            base=0, pattern=[[1, 256]], channel_multiplier=-1)
    c.inc(ins, "gI")
    c.g.memset(m1[:], 1.0)
    ins = c.g.affine_select(out=m1[:], in_=m1[:], compare_op=ALU.is_ge, fill=0.0,
                            base=-128, pattern=[[1, 256]], channel_multiplier=-1)
    c.inc(ins, "gI")
    gi_p3 = c.val("gI")

    conv_stage = a.alloc([2, 2048], BF16)
    convs = []
    for fc in range(8):
        for ob in range(2):
            convs.append((t["w1T"][fc * 128 : (fc + 1) * 128, ob * 2048 : (ob + 1) * 2048],
                          t["w1c"][:, fc, ob * 2048 : (ob + 1) * 2048]))
    for fc in range(8):
        for ob in range(2):
            convs.append((t["w2T"][fc * 128 : (fc + 1) * 128, ob * 2048 : (ob + 1) * 2048],
                          t["w2c"][:, fc, ob * 2048 : (ob + 1) * 2048]))
    for fc in range(32):
        for ob in range(2):
            convs.append((t["h1T"][fc * 128 : (fc + 1) * 128, ob * 2048 : (ob + 1) * 2048],
                          t["h1c"][:, fc, ob * 2048 : (ob + 1) * 2048]))
    for fc in range(32):
        for ob in range(2):
            convs.append((t["h2T"][fc * 128 : (fc + 1) * 128, ob * 2048 : (ob + 1) * 2048],
                          t["h2c"][:, fc, ob * 2048 : (ob + 1) * 2048]))
    for fc in range(32):
        convs.append((t["w3T"][fc * 128 : (fc + 1) * 128, :], t["w3c"][:, fc, :]))
    conv_marks = {}
    for i, (srcap, dstap) in enumerate(convs):
        n = srcap.shape[-1]
        if i >= 2:
            c.wait("gpsimd", "dCS", 16 * (i - 1))
        ins = c.g.dma_start(out=conv_stage[:, i % 2, 0 : n], in_=srcap)
        c.inc(ins, "dC", 16)
        c.wait("sync", "dC", 16 * (i + 1))
        ins = c.sy.dma_start(out=dstap, in_=conv_stage[:, i % 2, 0 : n])
        c.inc(ins, "dCS", 16)
        if i == 31:
            conv_marks["w1w2"] = c.val("dCS")
        if i == 95:
            conv_marks["h1"] = c.val("dCS")
        if i == 159:
            conv_marks["h2"] = c.val("dCS")
    conv_marks["w3"] = c.val("dCS")
    t["conv_marks"] = conv_marks

    c.v.memset(s_sb[:], 0.0)
    c.v.memset(s_bf[:], 0.0)
    c.v.memset(v_ext[:, :, :, 64 : 65], 1.0)
    c.v.drain()
    c.inc(c.v.nop(), "vSU")
    c.inc(c.v.nop(), "vEL")

    for cc in range(16):
        ti0 = 2 * cc
        for sub in range(2):
            g = cc * 2 + sub
            bk = g % 2
            if g == 0:
                c.wait("tensor", "vZ", p2_vz)
                c.wait("tensor", "vEL", c.val("vEL"))
                c.wait("tensor", "vSU", c.val("vSU"))
                c.wait("tensor", "gI", gi_p3)
            else:
                c.wait("tensor", "vXF", c.val("vXF"))
                if cc > 0:
                    c.wait("tensor", "vOT", c.val("vOT"))
            ins = None
            for fc in range(8):
                ins = c.te.transpose(pbt[bk][:, fc, :], t["stream"][:, ti0 + sub, fc * 128 : (fc + 1) * 128], t["ident"][:])
            c.inc(ins, "tXF")
            c.wait("vector", "tXF", c.val("tXF"))
            ins = c.v.tensor_copy(x_fm[:, :, sub * 128 : (sub + 1) * 128], pbt[bk][:, :, :])
            c.inc(ins, "vXF")

        c.wait("tensor", "vXF", c.val("vXF"))
        if cc == 0:
            c.wait("tensor", "dW", dw_p3)
        for sub in range(2):
            for which in range(3):
                j = (cc * 6 + sub * 3 + which) % 2
                base = which * 1024
                c.wait("tensor", "sQE", c.val("sQE"))
                c.wait("tensor", "vEL", c.val("vEL"))
                c.wait("tensor", "vPS", c.val("vPS"))
                ins = None
                for fc in range(8):
                    for nh in range(2):
                        ins = c.te.matmul(pp[j][:, nh * 512 : (nh + 1) * 512],
                                          x_fm[:, fc, sub * 128 : (sub + 1) * 128],
                                          wqkv[:, fc, base + nh * 512 : base + (nh + 1) * 512],
                                          start=(fc == 0), stop=(fc == 7))
                c.inc(ins, "tQKV")
                if which == 2:
                    c.wait("vector", "tQKV", c.val("tQKV"))
                    if cc > 0:
                        c.wait("vector", "tSU", c.val("tSU"))
                    ins = c.v.tensor_copy(v_ext[:, sub, :, 0 : 64],
                                          pp[j].rearrange("p (h d) -> p h d", d=64))
                    c.inc(ins, "vEL")
                else:
                    c.wait("scalar", "tQKV", c.val("tQKV"))
                    c.wait("scalar", "vEL", c.val("vEL"))
                    ins = c.s.copy(raw[:], pp[j])
                    c.inc(ins, "sQE")
                    c.wait("vector", "sQE", c.val("sQE"))
                    ins = c.v.tensor_scalar(elu_t[:], raw[:], 0.0, None, ALU.min)
                    c.inc(ins, "vEL")
                    c.wait("scalar", "vEL", c.val("vEL"))
                    ins = c.s.activation(elu_t[:], elu_t[:], AF.Exp)
                    c.inc(ins, "sQE")
                    if which == 0:
                        ins = c.v.tensor_scalar(raw[:], raw[:], 0.125, 0.0, ALU.mult, ALU.max)
                        c.inc(ins, "vEL")
                        c.v.drain()
                        c.wait("vector", "sQE", c.val("sQE"))
                        ins = c.v.scalar_tensor_tensor(q_tm[:, sub, :], elu_t[:], 0.125, raw[:], ALU.mult, ALU.add)
                        c.inc(ins, "vEL")
                    else:
                        ins = c.v.tensor_scalar(raw[:], raw[:], 0.0, None, ALU.max)
                        c.inc(ins, "vEL")
                        c.v.drain()
                        c.wait("vector", "sQE", c.val("sQE"))
                        ins = c.v.tensor_tensor(k_tm[:, sub, :], elu_t[:], raw[:], ALU.add)
                        c.inc(ins, "vEL")

        c.wait("tensor", "vEL", c.val("vEL"))
        tgt = 0
        for src_tm, dst64 in ((q_tm, q64), (k_tm, k64)):
            for sub in range(2):
                for hg in range(2):
                    g = cc * 8 + tgt
                    tgt += 1
                    bk = g % 2
                    c.wait("tensor", "vKT", c.val("vKT") - 1 if c.val("vKT") >= 2 else 0)
                    ins = None
                    for j in range(8):
                        h = hg * 8 + j
                        ins = c.te.transpose(pb[bk].bitcast(BF16)[0 : 64, j * 128 : (j + 1) * 128],
                                             src_tm[:, sub, h * 64 : (h + 1) * 64], t["ident"][:])
                    c.inc(ins, "tKT")
                    c.wait("vector", "tKT", c.val("tKT"))
                    ins = c.v.tensor_copy(dst64[0 : 64, hg * 8 : (hg + 1) * 8, sub * 128 : (sub + 1) * 128],
                                          pb[bk].bitcast(BF16)[0 : 64, :].rearrange("p (a b) -> p a b", a=8))
                    c.inc(ins, "vKT")

        c.wait("tensor", "vKT", c.val("vKT"))
        for h in range(16):
            c.wait("tensor", "vMA", c.val("vMA"))
            ins = None
            for mhalf in range(2):
                ins = c.te.matmul(p6[:, mhalf, :], k64[0 : 64, h, mhalf * 128 : (mhalf + 1) * 128],
                                  q64[0 : 64, h, :], start=True, stop=True)
            c.inc(ins, "tAT")
            c.wait("vector", "tAT", c.val("tAT"))
            for mhalf, mk_ in ((0, m0), (1, m1)):
                ins = c.v.tensor_tensor(atb[:, mhalf, :], p6[:, mhalf, :], mk_[:], ALU.mult)
            c.inc(ins, "vMA")
            c.wait("tensor", "vMA", c.val("vMA"))
            c.wait("tensor", "sNU", c.val("sNU"))
            ins = None
            for lhalf in range(2):
                sl = slice(lhalf * 128, (lhalf + 1) * 128)
                c.te.matmul(p7[:, lhalf, :], atb[:, 0, sl], v_ext[:, 0, h, :], start=True, stop=False)
                c.te.matmul(p7[:, lhalf, :], atb[:, 1, sl], v_ext[:, 1, h, :], start=False, stop=False)
                ins = c.te.matmul(p7[:, lhalf, :], q64[0 : 64, h, sl], s_bf[0 : 64, h, :], start=False, stop=True)
            c.inc(ins, "tNU")
            c.wait("scalar", "tNU", c.val("tNU"))
            ins = c.s.copy(num_sb[:, :, h, :], p7[:, :, :])
            c.inc(ins, "sNU")

        for hb in range(4):
            pS = pb[2 + hb][:, 0 : 260].rearrange("p (a b) -> p a b", a=4)
            ins = None
            for hh in range(4):
                h = hb * 4 + hh
                c.te.matmul(pS[0 : 64, hh, :], k_tm[:, 0, h * 64 : (h + 1) * 64], v_ext[:, 0, h, :],
                            start=True, stop=False)
                ins = c.te.matmul(pS[0 : 64, hh, :], k_tm[:, 1, h * 64 : (h + 1) * 64], v_ext[:, 1, h, :],
                                  start=False, stop=True)
            c.inc(ins, "tSU")
            c.wait("vector", "tSU", c.val("tSU"))
            ins = c.v.tensor_tensor(s_sb[0 : 64, hb * 4 : (hb + 1) * 4, :],
                                    s_sb[0 : 64, hb * 4 : (hb + 1) * 4, :], pS[0 : 64, :, :], ALU.add)
            c.v.drain()
            ins = c.v.tensor_copy(s_bf[0 : 64, hb * 4 : (hb + 1) * 4, :],
                                  s_sb[0 : 64, hb * 4 : (hb + 1) * 4, :])
            c.inc(ins, "vSU")

        c.wait("vector", "sNU", c.val("sNU"))
        ins = c.v.tensor_scalar(num_sb[:, :, :, 64 : 65], num_sb[:, :, :, 64 : 65], 1e-6, None, ALU.max)
        c.v.drain()
        ins = c.v.reciprocal(num_sb[:, :, :, 64 : 65], num_sb[:, :, :, 64 : 65])
        c.v.drain()
        rdb = num_sb[:, :, :, 64 : 65].broadcast_to([128, 2, 16, 64])
        ins = c.v.tensor_tensor(ov[:].rearrange("p a (h d) -> p a h d", d=64),
                                num_sb[:, :, :, 0 : 64], rdb, ALU.mult)
        c.inc(ins, "vDV")

        c.wait("tensor", "vDV", c.val("vDV"))
        for lhalf in range(2):
            g = cc * 2 + lhalf
            bk = g % 2
            if True:
                c.wait("tensor", "vOT", c.val("vOT") - 1 if c.val("vOT") >= 2 else 0)
            ins = None
            for fc in range(8):
                ins = c.te.transpose(pbt[bk][:, fc, :], ov[:, lhalf, fc * 128 : (fc + 1) * 128], t["ident"][:])
            c.inc(ins, "tOT")
            c.wait("vector", "tOT", c.val("tOT"))
            ins = c.v.tensor_copy(o_fm[:, :, lhalf * 128 : (lhalf + 1) * 128], pbt[bk][:, :, :])
            c.inc(ins, "vOT")

        c.wait("tensor", "vOT", c.val("vOT"))
        c.wait("tensor", "vSU", c.val("vSU"))
        for sub in range(2):
            j = sub
            ins = None
            for fc in range(8):
                for nh in range(2):
                    ins = c.te.matmul(pp[j][:, nh * 512 : (nh + 1) * 512],
                                      o_fm[:, fc, sub * 128 : (sub + 1) * 128],
                                      wout[:, fc, nh * 512 : (nh + 1) * 512],
                                      start=(fc == 0), stop=(fc == 7))
            c.inc(ins, "tWO")
        c.wait("vector", "tWO", c.val("tWO"))
        for sub in range(2):
            ins = c.v.tensor_tensor(z1k[:], t["stream"][:, ti0 + sub, :], pp[sub], ALU.add)
            c.inc(ins, "vPS")
            c.v.drain()
            c.v.reduce_sum(s1c[:, sub : sub + 1], z1k[:], axis=AX.X)
            c.v.tensor_tensor(zsq[:], z1k[:], z1k[:], ALU.mult)
            c.v.drain()
            c.v.reduce_sum(s2c[:, sub : sub + 1], zsq[:], axis=AX.X)
            c.v.drain()
            c.v.tensor_scalar(mean2[:, sub : sub + 1], s1c[:, sub : sub + 1], 1.0 / 1024.0, None, ALU.mult)
            c.v.drain()
            c.v.tensor_tensor(msq2[:, sub : sub + 1], mean2[:, sub : sub + 1], mean2[:, sub : sub + 1], ALU.mult)
            c.v.drain()
            c.v.scalar_tensor_tensor(var2[:, sub : sub + 1], s2c[:, sub : sub + 1], 1.0 / 1024.0,
                                     msq2[:, sub : sub + 1], ALU.mult, ALU.subtract)
            c.v.drain()
            c.v.tensor_scalar(var2[:, sub : sub + 1], var2[:, sub : sub + 1], EPS, None, ALU.add)
            c.v.drain()
            c.inc(c.v.nop(), "vST")
            c.wait("scalar", "vST", c.val("vST"))
            c.s.activation(lnv2[:, sub : sub + 1], var2[:, sub : sub + 1], AF.Ln)
            c.s.drain()
            c.s.activation(rstd2[:, sub : sub + 1], lnv2[:, sub : sub + 1], AF.Exp, scale=-0.5)
            c.s.drain()
            c.inc(c.s.nop(), "sRS")
            c.wait("vector", "sRS", c.val("sRS"))
            ins = c.v.tensor_scalar(t["stream"][:, ti0 + sub, :], z1k[:],
                                    mean2[:, sub : sub + 1], rstd2[:, sub : sub + 1],
                                    ALU.subtract, ALU.mult)
            c.inc(ins, "vZ")



def emit_p4(c, t):
    a = Arena(t["arena"], t["arena_bytes"])
    x_fm = a.alloc([8, 512], BF16)
    h_fm = a.alloc([32, 512], BF16)
    hh_fm = a.alloc([32, 512], BF16)
    w1p = a.alloc([8, 512], BF16)
    w2p = a.alloc([8, 512], BF16)
    h1p = a.alloc([2, 32, 128], BF16)
    w3p = a.alloc([32, 256], BF16)
    sgl = a.alloc([512], F32)
    scol = a.alloc([4, 32], F32)
    s3b = a.alloc([1024], F32)
    z2 = a.alloc([2, 1024], F32)
    zsq = a.alloc([1024], F32)
    st4 = a.alloc([8, 4], F32)
    assert a.off <= t["arena_bytes"], a.off
    s3r = z2[0:1, 0, :]
    onec = z2[0:1, 1, 0:128]

    s1c = st4[:, 0, :]
    s2c = st4[:, 1, :]
    mean4 = st4[:, 2, :]
    msq4 = st4[:, 3, :]
    var4 = st4[:, 4, :]
    lnv4 = st4[:, 5, :]
    rstd4 = st4[:, 6, :]

    pb = t["pb"]
    pbt = [pb[i].bitcast(BF16).rearrange("p (a b) -> p a b", a=8) for i in range(2)]
    cm = t["conv_marks"]
    p3_vz = c.val("vZ")

    c.wait("sync", "vZ", p3_vz)
    for i, nm in enumerate(("s1c", "s2c", "sh1c", "sh2c")):
        c.inc(c.sy.dma_start(out=scol[:, i, :], in_=t[nm]), "dX", 16)
    c.inc(c.sy.dma_start(out=s3r, in_=t["s3row"]), "dX", 16)
    dx_p4 = c.val("dX")
    c.wait("vector", "dX", dx_p4)
    c.v.memset(onec, 1.0)
    c.v.drain()
    c.inc(c.v.nop(), "vH")
    c.wait("tensor", "vZ", p3_vz)
    c.wait("tensor", "vH", c.val("vH"))
    c.wait("tensor", "dX", dx_p4)
    ins = None
    for nh in range(2):
        ins = c.te.matmul(t["pb_flat"][:, (6 + nh) * 512 : (7 + nh) * 512], onec,
                          s3r[:, nh * 512 : (nh + 1) * 512], start=True, stop=True)
    c.inc(ins, "tH")
    c.wait("vector", "tH", c.val("tH"))
    ins = c.v.tensor_copy(s3b[:], t["pb_flat"][:, 6 * 512 : 8 * 512])
    c.v.drain()
    c.inc(ins, "vH")
    c.inc(c.v.nop(), "vPS")

    wpieces = 0
    hpieces = []
    out_dmas = []

    for tt in range(8):
        ti0 = tt * 4
        for g4 in range(4):
            bk = g4 % 2
            c.wait("tensor", "vH", c.val("vH"))
            if tt > 0:
                c.wait("tensor", "vPS", c.val("vPS"))
            ins = None
            for fc in range(8):
                ins = c.te.transpose(pbt[bk][:, fc, :], t["stream"][:, ti0 + g4, fc * 128 : (fc + 1) * 128], t["ident"][:])
            c.inc(ins, "tH")
            c.wait("vector", "tH", c.val("tH"))
            ins = c.v.tensor_copy(x_fm[:, :, g4 * 128 : (g4 + 1) * 128], pbt[bk][:, :, :])
            c.inc(ins, "vH")

        for ob in range(8):
            wpieces += 1
            c.wait("gpsimd", "tH", c.val("tH"))
            if wpieces == 1:
                c.wait("gpsimd", "dCS", cm["w1w2"])
            c.inc(c.g.dma_start(out=w1p[:], in_=t["w1c"][:, :, ob * 512 : (ob + 1) * 512]), "dWH", 16)
            c.inc(c.g.dma_start(out=w2p[:], in_=t["w2c"][:, :, ob * 512 : (ob + 1) * 512]), "dWH", 16)
            c.wait("tensor", "dWH", c.val("dWH"))
            if ob == 0:
                c.wait("tensor", "vH", c.val("vH"))
            for oc in range(4):
                og = ob * 4 + oc
                gk = (oc % 2) * 2
                uk = gk + 1
                c.wait("tensor", "vH", c.val("vH") - 1 if (tt + og > 0) else 0)
                ins = None
                for fc in range(8):
                    ins = c.te.matmul(pb[gk], w1p[:, fc, oc * 128 : (oc + 1) * 128], x_fm[:, fc, :],
                                      start=(fc == 0), stop=(fc == 7))
                c.inc(ins, "tH")
                ins = None
                for fc in range(8):
                    ins = c.te.matmul(pb[uk], w2p[:, fc, oc * 128 : (oc + 1) * 128], x_fm[:, fc, :],
                                      start=(fc == 0), stop=(fc == 7))
                c.inc(ins, "tH")
                c.wait("scalar", "tH", c.val("tH") - 1)
                c.wait("scalar", "vH", c.val("vH"))
                ins = c.s.activation(sgl[:], pb[gk], AF.Silu, scale=scol[:, 0, og : og + 1])
                c.inc(ins, "sH")
                c.wait("vector", "tH", c.val("tH"))
                c.wait("vector", "sH", c.val("sH"))
                ins = c.v.scalar_tensor_tensor(h_fm[:, og, :], pb[uk], scol[:, 1, og : og + 1], sgl[:],
                                               ALU.mult, ALU.mult)
                c.inc(ins, "vH")

        for phase in range(2):
            srcbuf = h_fm if phase == 0 else hh_fm
            dstbuf = hh_fm if phase == 0 else h_fm
            cache = t["h1c"] if phase == 0 else t["h2c"]
            scidx = 2 + phase
            for ob in range(32):
                pidx = len(hpieces)
                buf = pidx % 2
                if pidx >= 2:
                    c.wait("sync", "tH", hpieces[pidx - 2])
                if tt == 0 and phase == 0 and ob == 0:
                    c.wait("sync", "dCS", cm["h1"])
                if tt == 0 and phase == 1 and ob == 0:
                    c.wait("sync", "dCS", cm["h2"])
                c.inc(c.sy.dma_start(out=h1p[:, buf], in_=cache[:, :, ob * 128 : (ob + 1) * 128]), "dK", 16)
                c.wait("tensor", "dK", c.val("dK"))
                if ob == 0:
                    c.wait("tensor", "vH", c.val("vH"))
                    c.wait("tensor", "sH", c.val("sH"))
                og = ob
                bk45 = 4 + (og % 2)
                if tt + phase + og > 0:
                    c.wait("tensor", "sH", c.val("sH") - 1)
                ins = None
                for ic in range(32):
                    ins = c.te.matmul(pb[bk45], h1p[:, buf, ic, :], srcbuf[:, ic, :],
                                      start=(ic == 0), stop=(ic == 31))
                c.inc(ins, "tH")
                hpieces.append(c.val("tH"))
                c.wait("scalar", "tH", c.val("tH"))
                ins = c.s.activation(dstbuf[:, og, :], pb[bk45], AF.Gelu, scale=scol[:, scidx, og : og + 1])
                c.inc(ins, "sH")

        for qtr in range(4):
            c.wait("sync", "tH", c.val("tH"))
            if tt == 0 and qtr == 0:
                c.wait("sync", "dCS", cm["w3"])
            c.inc(c.sy.dma_start(out=w3p[:], in_=t["w3c"][:, :, qtr * 256 : (qtr + 1) * 256]), "dK", 16)
            c.wait("tensor", "dK", c.val("dK"))
            if qtr == 0:
                c.wait("tensor", "sH", c.val("sH"))
                c.wait("tensor", "vPS", c.val("vPS"))
            for sub in range(4):
                ps_q = t["pb_flat"][:, (2 * sub) * 512 + qtr * 256 : (2 * sub) * 512 + (qtr + 1) * 256]
                ins = None
                for ic in range(32):
                    ins = c.te.matmul(ps_q, h_fm[:, ic, sub * 128 : (sub + 1) * 128],
                                      w3p[:, ic, :], start=(ic == 0), stop=(ic == 31))
                c.inc(ins, "tH")
        for pair in range(2):
            for k in range(2):
                sub = pair * 2 + k
                ti = ti0 + sub
                psflat = t["pb_flat"][:, (2 * sub) * 512 : (2 * sub + 2) * 512]
                c.wait("vector", "tH", c.val("tH"))
                if len(out_dmas) >= 2:
                    c.wait("vector", "dO", out_dmas[-2])
                ins = c.v.tensor_tensor(z2[:, k, :], psflat, s3b[:], ALU.mult)
                c.inc(ins, "vPS")
                c.v.drain()
                ins = c.v.tensor_tensor(z2[:, k, :], z2[:, k, :], t["stream"][:, ti, :], ALU.add)
                c.v.drain()
                c.v.reduce_sum(s1c[:, sub : sub + 1], z2[:, k, :], axis=AX.X)
                c.v.tensor_tensor(zsq[:], z2[:, k, :], z2[:, k, :], ALU.mult)
                c.v.drain()
                c.v.reduce_sum(s2c[:, sub : sub + 1], zsq[:], axis=AX.X)
            c.v.drain()
            sl = slice(pair * 2, pair * 2 + 2)
            c.v.tensor_scalar(mean4[:, sl], s1c[:, sl], 1.0 / 1024.0, None, ALU.mult)
            c.v.drain()
            c.v.tensor_tensor(msq4[:, sl], mean4[:, sl], mean4[:, sl], ALU.mult)
            c.v.drain()
            c.v.scalar_tensor_tensor(var4[:, sl], s2c[:, sl], 1.0 / 1024.0, msq4[:, sl], ALU.mult, ALU.subtract)
            c.v.drain()
            c.v.tensor_scalar(var4[:, sl], var4[:, sl], EPS, None, ALU.add)
            c.v.drain()
            c.inc(c.v.nop(), "vST")
            c.wait("scalar", "vST", c.val("vST"))
            c.s.activation(lnv4[:, sl], var4[:, sl], AF.Ln)
            c.s.drain()
            c.s.activation(rstd4[:, sl], lnv4[:, sl], AF.Exp, scale=-0.5)
            c.s.drain()
            c.inc(c.s.nop(), "sRS")
            c.wait("vector", "sRS", c.val("sRS"))
            for k in range(2):
                sub = pair * 2 + k
                ti = ti0 + sub
                ins = c.v.tensor_scalar(z2[:, k, :], z2[:, k, :],
                                        mean4[:, sub : sub + 1], rstd4[:, sub : sub + 1],
                                        ALU.subtract, ALU.mult)
                c.v.drain()
                c.inc(c.v.nop(), "vZ")
                c.wait("scalar", "vZ", c.val("vZ"))
                ins = c.s.dma_start(out=t["out"][ti * 128 : (ti + 1) * 128, :], in_=z2[:, k, :])
                out_dmas.append(c.inc(ins, "dO", 16))


ARENA_BYTES = 146944


def build_nc(upto=4):
    nc = bass.Bass("TRN2", target_bir_lowering=False)
    t = {}
    t["arena_bytes"] = ARENA_BYTES

    t["x"] = nc.dram_tensor("x", [T, D], F32, kind="ExternalInput").ap()
    t["wqT"] = nc.dram_tensor("wqT", [D, D], F32, kind="ExternalInput").ap()
    t["wkT"] = nc.dram_tensor("wkT", [D, D], F32, kind="ExternalInput").ap()
    t["wvT"] = nc.dram_tensor("wvT", [D, D], F32, kind="ExternalInput").ap()
    t["out"] = nc.dram_tensor("out", [T, D], F32, kind="ExternalOutput").ap()
    t["gate_w"] = nc.dram_tensor("gate_w", [D, 8], F32, kind="ExternalInput").ap()
    t["sh_gate"] = nc.dram_tensor("sh_gate", [D, 2048], F32, kind="ExternalInput").ap()
    t["sh_up"] = nc.dram_tensor("sh_up", [D, 2048], F32, kind="ExternalInput").ap()
    t["sh_down"] = nc.dram_tensor("sh_down", [2048, D], F32, kind="ExternalInput").ap()
    t["rt_gate"] = nc.dram_tensor("rt_gate", [8, D, 2048], F32, kind="ExternalInput").ap()
    t["rt_up"] = nc.dram_tensor("rt_up", [8, D, 2048], F32, kind="ExternalInput").ap()
    t["rt_down"] = nc.dram_tensor("rt_down", [8, 2048, D], F32, kind="ExternalInput").ap()
    t["wqkvT"] = nc.dram_tensor("wqkvT", [D, 3072], F32, kind="ExternalInput").ap()
    t["w1T"] = nc.dram_tensor("w1T", [D, 4096], F32, kind="ExternalInput").ap()
    t["w2T"] = nc.dram_tensor("w2T", [D, 4096], F32, kind="ExternalInput").ap()
    t["h1T"] = nc.dram_tensor("h1T", [4096, 4096], F32, kind="ExternalInput").ap()
    t["h2T"] = nc.dram_tensor("h2T", [4096, 4096], F32, kind="ExternalInput").ap()
    t["w3T"] = nc.dram_tensor("w3T", [4096, D], F32, kind="ExternalInput").ap()
    t["s1c"] = nc.dram_tensor("s1c", [128, 32], F32, kind="ExternalInput").ap()
    t["s2c"] = nc.dram_tensor("s2c", [128, 32], F32, kind="ExternalInput").ap()
    t["sh1c"] = nc.dram_tensor("sh1c", [128, 32], F32, kind="ExternalInput").ap()
    t["sh2c"] = nc.dram_tensor("sh2c", [128, 32], F32, kind="ExternalInput").ap()
    t["s3row"] = nc.dram_tensor("s3row", [1, D], F32, kind="ExternalInput").ap()
    t["w1c"] = nc.dram_tensor("w1c", [128, 8, 4096], BF16, kind="Internal").ap()
    t["w2c"] = nc.dram_tensor("w2c", [128, 8, 4096], BF16, kind="Internal").ap()
    t["h1c"] = nc.dram_tensor("h1c", [128, 32, 4096], BF16, kind="Internal").ap()
    t["h2c"] = nc.dram_tensor("h2c", [128, 32, 4096], BF16, kind="Internal").ap()
    t["w3c"] = nc.dram_tensor("w3c", [128, 32, 1024], BF16, kind="Internal").ap()
    t["woutT"] = nc.dram_tensor("woutT", [D, D], F32, kind="ExternalInput").ap()

    t["q_spill"] = nc.dram_tensor("q_spill", [8, 128, 8, 512], BF16, kind="Internal").ap()
    t["k_spill"] = nc.dram_tensor("k_spill", [8, 128, 8, 512], BF16, kind="Internal").ap()
    t["v_spill"] = nc.dram_tensor("v_spill", [8, 128, 4, 1024], BF16, kind="Internal").ap()

    import contextlib
    es = contextlib.ExitStack()
    with es:
        t["stream"] = es.enter_context(nc.sbuf_tensor("stream", [128, NT, 1024], BF16))
        t["ident"] = es.enter_context(nc.sbuf_tensor("ident", [128, 128], BF16))
        arena = es.enter_context(nc.sbuf_tensor("arena", [128, ARENA_BYTES // 2], BF16))
        t["arena"] = arena
        pbs = es.enter_context(nc.psum_tensor("pball", [128, 8, 512], F32))
        t["pb"] = [pbs[:, i, :] for i in range(8)]
        t["pb_flat"] = pbs[:].rearrange("p a b -> p (a b)")
        sems = {}
        for name in SEMS:
            sems[name] = es.enter_context(nc.semaphore(name))
        block = es.enter_context(nc.Block())

        def run_pass(eng_name):
            def f(eng):
                cx = Ctx(nc, sems, eng_name, eng)
                emit_program(cx, t, upto=upto)
            return f

        block.sync(run_pass("sync"))
        block.vector(run_pass("vector"))
        block.scalar(run_pass("scalar"))
        block.tensor(run_pass("tensor"))
        block.gpsimd(run_pass("gpsimd"))
    return nc


def host_inputs_per_core(core, inp):
    asc = np.ascontiguousarray
    return {
        "x": asc(inp["L0_attn_x"][core]),
        "wqT": asc(inp["L0_attn_Wq"].T),
        "wkT": asc(inp["L0_attn_Wk"].T),
        "wvT": asc(inp["L0_attn_Wv"].T),
        "gate_w": asc(inp["L0_ffn_gate_w"]),
        "sh_gate": asc(inp["L0_ffn_sh_gate"]),
        "sh_up": asc(inp["L0_ffn_sh_up"]),
        "sh_down": asc(inp["L0_ffn_sh_down"]),
        "rt_gate": asc(inp["L0_ffn_rt_gate"]),
        "rt_up": asc(inp["L0_ffn_rt_up"]),
        "rt_down": asc(inp["L0_ffn_rt_down"]),
        "wqkvT": asc(inp["L1_attn_W_qkv"].T),
        "woutT": asc(inp["L1_attn_W_out"].T),
        "w1T": asc(inp["L1_ffn_w1_w"].T),
        "w2T": asc(inp["L1_ffn_w2_w"].T),
        "h1T": asc(inp["L1_ffn_h1_w"].T),
        "h2T": asc(inp["L1_ffn_h2_w"].T),
        "w3T": asc(inp["L1_ffn_w3_w"].T),
        "s1c": asc(inp["L1_ffn_w1_s"].reshape(32, 128).T),
        "s2c": asc(inp["L1_ffn_w2_s"].reshape(32, 128).T),
        "sh1c": asc(inp["L1_ffn_h1_s"].reshape(32, 128).T),
        "sh2c": asc(inp["L1_ffn_h2_s"].reshape(32, 128).T),
        "s3row": asc(inp["L1_ffn_w3_s"].reshape(1, 1024)),
    }


def kernel(**inputs):
    inputs = {k: np.asarray(v) for k, v in inputs.items()}
    nc = build_nc(upto=4)
    in_maps = [host_inputs_per_core(cc, inputs) for cc in range(8)]
    res = run_bass_kernel_spmd(nc, in_maps, core_ids=list(range(8)))
    return np.stack([res.results[cc]["out"] for cc in range(8)], axis=0)


# version 03
# speedup vs baseline: 1.0237x; 1.0237x over previous
import math
import numpy as np
import concourse.bass as bass
import concourse.mybir as mybir
from concourse.bass_utils import run_bass_kernel_spmd

F32 = mybir.dt.float32
BF16 = mybir.dt.bfloat16
ALU = mybir.AluOpType
AX = mybir.AxisListType
AF = mybir.ActivationFunctionType

D = 1024
T = 4096
NT = 32
EPS = 1e-5

SEMS = [
    "dX", "dQ", "dK", "dV", "dW", "dSp", "dO",
    "gI",
    "vX", "vF", "sE", "tTr", "tQK",
    "tSC", "sSC", "vCM", "vB", "sEX", "vRD", "tTW", "vWT", "tPV", "vPS",
    "sRS", "vST", "vZ",
    "tG", "sG", "vMK", "vXS", "tXF", "vXF", "tGU", "sSI", "vAC", "tDN", "vYA",
    "tQKV", "sQE", "vEL", "tAT", "vMA", "tNU", "sNU", "vDV", "tKT", "vKT",
    "tSU", "vSU", "tOT", "vOT", "tWO",
    "tH", "sH", "vH", "dWH", "dC", "dCS",
]


class _Null:
    def __getattr__(self, k):
        def f(*a, **kw):
            return None
        return f


_NULL = _Null()


class Ctx:
    def __init__(self, nc, sems, eng_name, eng):
        self.nc = nc
        self.sems = sems
        self.e = eng_name
        self.eng = eng
        self.cnt = {k: 0 for k in sems}
        self.v = eng if eng_name == "vector" else _NULL
        self.s = eng if eng_name == "scalar" else _NULL
        self.te = eng if eng_name == "tensor" else _NULL
        self.g = eng if eng_name == "gpsimd" else _NULL
        self.sy = eng if eng_name == "sync" else _NULL

    def wait(self, engname, sem, val=None):
        if val is None:
            val = self.cnt[sem]
        if self.e == engname:
            self.eng.wait_ge(self.sems[sem], val)

    def inc(self, ins, sem, k=1):
        self.cnt[sem] += k
        if ins is not None:
            ins.then_inc(self.sems[sem], k)
        return self.cnt[sem]

    def val(self, sem):
        return self.cnt[sem]


def _carve(arena, byte_off, shape, dtype):
    nelem = int(np.prod(shape))
    esz = 4 if dtype == F32 else 2
    assert byte_off % 4 == 0
    start2 = byte_off // 2
    ap = arena[:, start2 : start2 + nelem * esz // 2]
    if dtype == F32:
        ap = ap.bitcast(F32)
    if len(shape) == 1:
        return ap
    spec = "p (" + " ".join(f"a{i}" for i in range(len(shape))) + ") -> p " + " ".join(
        f"a{i}" for i in range(len(shape))
    )
    return ap.rearrange(spec, **{f"a{i}": s for i, s in enumerate(shape)})


class Arena:
    def __init__(self, arena_ap, nbytes, off=0):
        self.ap = arena_ap
        self.nbytes = nbytes
        self.off = off

    def alloc(self, shape, dtype):
        esz = 4 if dtype == F32 else 2
        nelem = int(np.prod(shape))
        sz = nelem * esz
        off = (self.off + 3) // 4 * 4
        assert off + sz <= self.nbytes, f"arena overflow: {off}+{sz} > {self.nbytes}"
        v = _carve(self.ap, off, shape, dtype)
        self.off = off + sz
        return v


TILE_LAYOUT = "(s p) f -> p s f"


def emit_program(c, t, upto=4):
    emit_init(c, t)
    emit_p0(c, t)
    emit_p1(c, t)
    if upto >= 2:
        emit_p2(c, t)
    if upto >= 3:
        emit_p3(c, t)
    if upto >= 4:
        emit_p4(c, t)
    else:
        emit_stream_out(c, t)
    c.wait("scalar", "dO")


def emit_init(c, t):
    c.g.memset(t["ident"][:], 0.0)
    ins = c.g.affine_select(
        out=t["ident"][:], in_=t["ident"][:], compare_op=ALU.not_equal,
        fill=1.0, base=0, pattern=[[-1, 128]], channel_multiplier=1)
    c.inc(ins, "gI")



def emit_p0(c, t):
    a = Arena(t["arena"], t["arena_bytes"])
    wq = a.alloc([8, 1024], BF16)
    wk = a.alloc([8, 1024], BF16)
    wv = a.alloc([8, 1024], BF16)
    xt = a.alloc([2, 4, 1024], F32)
    xb = a.alloc([4, 1024], BF16)
    xfm = a.alloc([8, 512], BF16)
    qst = a.alloc([8, 512], BF16)
    kst = a.alloc([8, 512], BF16)
    vst = a.alloc([4, 1024], BF16)
    assert a.off <= t["arena_bytes"], a.off

    x = t["x"]
    pb = t["pb"]
    pbt = [pb[i].bitcast(BF16).rearrange("p (a b) -> p a b", a=8) for i in range(2)]

    for w_ap, dst in ((t["wqT"], wq), (t["wkT"], wk), (t["wvT"], wv)):
        src = w_ap.rearrange("(fc p) o -> p fc o", p=128)
        c.inc(c.g.dma_start(out=dst[:], in_=src), "dW", 16)

    xb_done = []
    tr_done = []
    st_done = []
    mm_groups = [0]

    for m in range(8):
        buf = m % 2
        if m >= 2:
            c.wait("sync", "vX", xb_done[m - 2])
        ins = c.sy.dma_start(out=xt[:, buf], in_=x[m * 512 : (m + 1) * 512, :].rearrange(TILE_LAYOUT, p=128))
        c.inc(ins, "dX", 16)

        c.wait("vector", "dX", 16 * (m + 1))
        if m >= 1:
            c.wait("vector", "tTr", tr_done[m - 1])
        ins = c.v.tensor_copy(xb[:], xt[:, buf])
        xb_done.append(c.inc(ins, "vX"))

        c.wait("tensor", "vX", xb_done[m])
        if m == 0:
            c.wait("tensor", "gI", 1)
        for g in range(4):
            bk = g % 2
            if m * 4 + g >= 2:
                c.wait("tensor", "vF", c.val("vF") - 1)
            ins = None
            for fc in range(8):
                ins = c.te.transpose(pbt[bk][:, fc, :], xb[:, g, fc * 128 : (fc + 1) * 128], t["ident"][:])
            c.inc(ins, "tTr")
            c.wait("vector", "tTr", c.val("tTr"))
            ins = c.v.tensor_copy(xfm[:, :, g * 128 : (g + 1) * 128], pbt[bk][:, :, :])
            c.inc(ins, "vF")
        tr_done.append(c.val("tTr"))

        c.wait("tensor", "vF", c.val("vF"))
        if m == 0:
            c.wait("tensor", "dW", 48)
        if m >= 1:
            c.wait("tensor", "dSp", st_done[m - 1])

        def mmgroup(emit_mms, evac_dst):
            i = mm_groups[0]
            mm_groups[0] += 1
            bank = 2 + (i % 4)
            if i >= 4:
                c.wait("tensor", "sE", i - 3)
            ins = emit_mms(pb[bank])
            c.inc(ins, "tQK")
            c.wait("scalar", "tQK", c.val("tQK"))
            ins = c.s.copy(evac_dst, pb[bank])
            c.inc(ins, "sE")

        for wsb, stage in ((wq, qst), (wk, kst)):
            for oc in range(8):
                def mk(ps, wsb=wsb, oc=oc):
                    ins = None
                    for fc in range(8):
                        ins = c.te.matmul(ps, wsb[:, fc, oc * 128 : (oc + 1) * 128], xfm[:, fc, :],
                                          start=(fc == 0), stop=(fc == 7))
                    return ins
                mmgroup(mk, stage[:, oc, :])
        for s in range(4):
            for half in range(2):
                def mk(ps, s=s, half=half):
                    ins = None
                    for fc in range(8):
                        ins = c.te.matmul(ps, xfm[:, fc, s * 128 : (s + 1) * 128],
                                          wv[:, fc, half * 512 : (half + 1) * 512],
                                          start=(fc == 0), stop=(fc == 7))
                    return ins
                mmgroup(mk, vst[:, s, half * 512 : (half + 1) * 512])

        c.s.drain()
        c.inc(c.s.dma_start(out=t["q_spill"][m], in_=qst[:]), "dSp", 16)
        c.inc(c.s.dma_start(out=t["k_spill"][m], in_=kst[:]), "dSp", 16)
        c.inc(c.s.dma_start(out=t["v_spill"][m], in_=vst[:]), "dSp", 16)
        st_done.append(c.val("dSp"))



def emit_p1(c, t):
    a = Arena(t["arena"], t["arena_bytes"])
    qbuf = a.alloc([8, 512], BF16)
    kch = a.alloc([2, 8, 512], BF16)
    vch = a.alloc([2, 4, 1024], BF16)
    scb = a.alloc([4, 8, 512], BF16)
    wT = a.alloc([32, 512], BF16)
    xres = a.alloc([4, 1024], F32)
    z2 = a.alloc([2, 1024], F32)
    zsq = a.alloc([1024], F32)
    cmax = a.alloc([4, 8], F32)
    st8 = a.alloc([12, 4], F32)
    assert a.off <= t["arena_bytes"], a.off

    rmneg = st8[:, 0, :]
    bias4 = st8[:, 1, :]
    den4 = st8[:, 2, :]
    rden4 = st8[:, 3, :]
    s1c = st8[:, 4, :]
    s2c = st8[:, 5, :]
    mean4 = st8[:, 6, :]
    msq4 = st8[:, 7, :]
    var4 = st8[:, 8, :]
    lnv4 = st8[:, 9, :]
    rstd4 = st8[:, 10, :]

    pb = t["pb"]
    pbt = [pb[i].bitcast(BF16).rearrange("p (a b) -> p a b", a=8) for i in range(2)]
    x = t["x"]

    p0_dsp = c.val("dSp")
    p0_dx = c.val("dX")
    qk_used = {}
    kch_used = {}
    vch_used = {}
    epi_done = []
    ps_rel = []

    for m in range(8):
        if m == 0:
            c.wait("sync", "dSp", p0_dsp)
        if m >= 1:
            c.wait("sync", "tSC", qk_used[m - 1])
        ins = c.sy.dma_start(out=qbuf[:], in_=t["q_spill"][m])
        c.inc(ins, "dQ", 16)
        if m >= 1:
            c.wait("sync", "vZ", epi_done[m - 1])
        ins = c.sy.dma_start(out=xres[:], in_=x[m * 512 : (m + 1) * 512, :].rearrange(TILE_LAYOUT, p=128))
        c.inc(ins, "dX", 16)

        if m >= 1:
            c.wait("tensor", "vPS", ps_rel[m - 1])
        for kc in range(8):
            kb = kc % 2
            gidx = m * 8 + kc
            if gidx >= 2:
                c.wait("sync", "tSC", kch_used[gidx - 2])
            ins = c.sy.dma_start(out=kch[:, kb], in_=t["k_spill"][kc])
            c.inc(ins, "dK", 16)
            c.wait("tensor", "dK", 16 * (gidx + 1))
            if kc == 0:
                c.wait("tensor", "dQ", 16 * (m + 1))
            for s in range(4):
                bank = (0 if kc % 2 == 0 else 4) + s
                if gidx >= 2:
                    c.wait("tensor", "sSC", c.val("sSC") - 7)
                ins = None
                for fc in range(8):
                    ins = c.te.matmul(pb[bank], qbuf[:, fc, s * 128 : (s + 1) * 128],
                                      kch[:, kb, fc, :], start=(fc == 0), stop=(fc == 7))
                c.inc(ins, "tSC")
                c.wait("scalar", "tSC", c.val("tSC"))
                ins = c.s.copy(scb[:, s, kc, :], pb[bank])
                c.inc(ins, "sSC")
                c.wait("vector", "sSC", c.val("sSC"))
                ins = c.v.reduce_max(cmax[:, s, kc : kc + 1], scb[:, s, kc, :], axis=AX.X)
                c.inc(ins, "vCM")
            kch_used[gidx] = c.val("tSC")
        qk_used[m] = c.val("tSC")

        c.wait("vector", "vCM", c.val("vCM"))
        for s in range(4):
            c.v.reduce_max(rmneg[:, s : s + 1], cmax[:, s, :], axis=AX.X, negate=True)
        c.v.drain()
        c.v.tensor_scalar(bias4[:], rmneg[:], 1.0 / 32.0, None, ALU.mult)
        c.v.drain()
        c.inc(c.v.nop(), "vB")

        c.wait("scalar", "vB", c.val("vB"))
        for s in range(4):
            ins = c.s.activation(scb[:, s].rearrange("p a b -> p (a b)"),
                                 scb[:, s].rearrange("p a b -> p (a b)"),
                                 AF.Exp, bias=bias4[:, s : s + 1], scale=1.0 / 32.0,
                                 accum_out=den4[:, s : s + 1])
            c.inc(ins, "sEX")

        c.wait("vector", "sEX", c.val("sEX"))
        c.v.reciprocal(rden4[:], den4[:])
        c.v.drain()
        c.inc(c.v.nop(), "vRD")

        sex_base = c.val("sEX") - 4
        for s in range(4):
            c.wait("tensor", "sEX", sex_base + s + 1)
            for kc in range(8):
                g = s * 8 + kc
                bk = g % 2
                if g >= 2:
                    c.wait("tensor", "vWT", c.val("vWT") - 1)
                ins = None
                for j in range(4):
                    ins = c.te.transpose(pbt[bk][:, j, :], scb[:, s, kc, j * 128 : (j + 1) * 128], t["ident"][:])
                c.inc(ins, "tTW")
                c.wait("vector", "tTW", c.val("tTW"))
                ins = c.v.tensor_copy(wT[:, kc * 4 : kc * 4 + 4, s * 128 : (s + 1) * 128], pbt[bk][:, 0:4, :])
                c.inc(ins, "vWT")

        c.wait("tensor", "vWT", c.val("vWT"))
        for vc in range(8):
            vb = vc % 2
            gidx = m * 8 + vc
            if gidx >= 2:
                c.wait("sync", "tPV", vch_used[gidx - 2])
            ins = c.sy.dma_start(out=vch[:, vb], in_=t["v_spill"][vc])
            c.inc(ins, "dV", 16)
            c.wait("tensor", "dV", 16 * (gidx + 1))
            ins = None
            for s in range(4):
                for j in range(4):
                    kcg = vc * 4 + j
                    for half in range(2):
                        ins = c.te.matmul(pb[2 * s + half],
                                          wT[:, kcg, s * 128 : (s + 1) * 128],
                                          vch[:, vb, j, half * 512 : (half + 1) * 512],
                                          start=(kcg == 0), stop=(kcg == 31))
            c.inc(ins, "tPV")
            vch_used[gidx] = c.val("tPV")

        c.wait("vector", "dX", p0_dx + 16 * (m + 1))
        c.wait("vector", "vRD", c.val("vRD"))
        c.wait("vector", "tPV", c.val("tPV"))
        for pair in range(2):
            for k in range(2):
                s = pair * 2 + k
                psflat = t["pb_flat"][:, (2 * s) * 512 : (2 * s + 2) * 512]
                ins = c.v.scalar_tensor_tensor(z2[:, k, :], psflat, rden4[:, s : s + 1],
                                               xres[:, s, :], ALU.mult, ALU.add)
                c.inc(ins, "vPS")
                c.v.drain()
                c.v.reduce_sum(s1c[:, s : s + 1], z2[:, k, :], axis=AX.X)
                c.v.tensor_tensor(zsq[:], z2[:, k, :], z2[:, k, :], ALU.mult)
                c.v.drain()
                c.v.reduce_sum(s2c[:, s : s + 1], zsq[:], axis=AX.X)
            c.v.drain()
            sl = slice(pair * 2, pair * 2 + 2)
            c.v.tensor_scalar(mean4[:, sl], s1c[:, sl], 1.0 / 1024.0, None, ALU.mult)
            c.v.drain()
            c.v.tensor_tensor(msq4[:, sl], mean4[:, sl], mean4[:, sl], ALU.mult)
            c.v.drain()
            c.v.scalar_tensor_tensor(var4[:, sl], s2c[:, sl], 1.0 / 1024.0, msq4[:, sl],
                                     ALU.mult, ALU.subtract)
            c.v.drain()
            c.v.tensor_scalar(var4[:, sl], var4[:, sl], EPS, None, ALU.add)
            c.v.drain()
            c.inc(c.v.nop(), "vST")
            c.wait("scalar", "vST", c.val("vST"))
            c.s.activation(lnv4[:, sl], var4[:, sl], AF.Ln)
            c.s.drain()
            c.s.activation(rstd4[:, sl], lnv4[:, sl], AF.Exp, scale=-0.5)
            c.s.drain()
            c.inc(c.s.nop(), "sRS")
            c.wait("vector", "sRS", c.val("sRS"))
            for k in range(2):
                s = pair * 2 + k
                ins = c.v.tensor_scalar(t["stream"][:, m * 4 + s, :], z2[:, k, :],
                                        mean4[:, s : s + 1], rstd4[:, s : s + 1],
                                        ALU.subtract, ALU.mult)
                c.inc(ins, "vZ")
        epi_done.append(c.val("vZ"))
        ps_rel.append(c.val("vPS"))


def emit_stream_out(c, t):
    a = Arena(t["arena"], t["arena_bytes"])
    stg = a.alloc([2, 1024], F32)
    do0 = c.val("dO")
    for ti in range(NT):
        b = ti % 2
        if ti >= 2:
            c.wait("vector", "dO", do0 + 16 * (ti - 1))
        c.wait("vector", "vZ", c.val("vZ"))
        ins = c.v.tensor_copy(stg[:, b, :], t["stream"][:, ti, :])
        c.inc(ins, "vZ")
        c.wait("scalar", "vZ", c.val("vZ"))
        ins = c.s.dma_start(out=t["out"][ti * 128 : (ti + 1) * 128, :], in_=stg[:, b, :])
        c.inc(ins, "dO", 16)



def emit_p2(c, t):
    a = Arena(t["arena"], t["arena_bytes"])
    x_fm = a.alloc([8, 1024], BF16)
    xs_fm = a.alloc([8, 1024], BF16)
    y_acc = a.alloc([8, 1024], F32)
    gate_p = a.alloc([2, 8, 512], BF16)
    up_p = a.alloc([2, 8, 512], BF16)
    down_p = a.alloc([2, 4, 1024], BF16)
    act_q = a.alloc([2, 4, 512], BF16)
    sg = a.alloc([512], F32)
    gw = a.alloc([8, 8], BF16)
    lg8 = a.alloc([8, 8], F32)
    rm8 = a.alloc([8], F32)
    sc8 = a.alloc([8], F32)
    masks8 = a.alloc([8, 8], F32)
    xst = a.alloc([1024], BF16)
    zsq = a.alloc([1024], F32)
    st16 = a.alloc([8, 8], F32)
    assert a.off <= t["arena_bytes"], a.off

    s1c = st16[:, 0, :]
    s2c = st16[:, 1, :]
    mean8 = st16[:, 2, :]
    msq8 = st16[:, 3, :]
    var8 = st16[:, 4, :]
    lnv8 = st16[:, 5, :]
    rstd8 = st16[:, 6, :]

    pb = t["pb"]
    pbt = [pb[2 + i].bitcast(BF16).rearrange("p (a b) -> p a b", a=8) for i in range(2)]

    p1_tpv = c.val("tPV")
    p1_vz = c.val("vZ")

    c.wait("gpsimd", "tPV", p1_tpv)
    c.wait("gpsimd", "vZ", p1_vz)
    ins = c.g.dma_start(out=gw[:], in_=t["gate_w"].rearrange("(fc p) e -> p fc e", p=128))
    c.inc(ins, "dW", 16)
    gw_dw = c.val("dW")

    piece_used = []

    for qq in range(4):
        tiles = [qq * 8 + su for su in range(8)]
        c.wait("tensor", "vZ", p1_vz)

        for su in range(8):
            ti = tiles[su]
            gidx = qq * 16 + su
            bk = gidx % 2
            if qq + su > 0:
                c.wait("tensor", "vXF", c.val("vXF"))
            ins = None
            for fc in range(8):
                ins = c.te.transpose(pbt[bk][:, fc, :], t["stream"][:, ti, fc * 128 : (fc + 1) * 128], t["ident"][:])
            c.inc(ins, "tXF")
            c.wait("vector", "tXF", c.val("tXF"))
            c.v.nop(cycle_cnt=2000)
            ins = c.v.tensor_copy(x_fm[:, :, su * 128 : (su + 1) * 128], pbt[bk][:, :, :])
            c.inc(ins, "vXF")
            gbank = 6 + (su % 2)
            c.wait("tensor", "vXF", c.val("vXF"))
            if su >= 2:
                c.wait("tensor", "sG", c.val("sG") - 1)
            if qq == 0 and su == 0:
                c.wait("tensor", "dW", gw_dw)
            ins = None
            for fc in range(8):
                ins = c.te.matmul(pb[gbank][:, 0:8], x_fm[:, fc, su * 128 : (su + 1) * 128], gw[:, fc, :],
                                  start=(fc == 0), stop=(fc == 7))
            c.inc(ins, "tG")
            c.wait("scalar", "tG", c.val("tG"))
            ins = c.s.copy(lg8[:, su, :], pb[gbank][:, 0:8])
            c.inc(ins, "sG")
        c.wait("vector", "sG", c.val("sG"))
        for su in range(8):
            c.v.reduce_max(rm8[:, su : su + 1], lg8[:, su, :], axis=AX.X)
        c.v.drain()
        for su in range(8):
            c.v.tensor_scalar(masks8[:, su, :], lg8[:, su, :], rm8[:, su : su + 1], None, ALU.is_equal)
        c.v.drain()
        c.inc(c.v.nop(), "vMK")
        c.wait("scalar", "vMK", c.val("vMK"))
        ins = c.s.activation(sc8[:], rm8[:], AF.Sigmoid)
        c.s.drain()
        c.inc(c.s.nop(), "sG")

        c.wait("vector", "sG", c.val("sG"))
        for su in range(8):
            ti = tiles[su]
            ins = c.v.tensor_scalar(xst[:], t["stream"][:, ti, :], sc8[:, su : su + 1], None, ALU.mult)
            c.inc(ins, "vXS")
            gidx = su
            bk = su % 2
            c.wait("tensor", "vXS", c.val("vXS"))
            if su >= 1:
                c.wait("tensor", "vXF", c.val("vXF"))
            ins = None
            for fc in range(8):
                ins = c.te.transpose(pbt[bk][:, fc, :], xst[:, fc * 128 : (fc + 1) * 128], t["ident"][:])
            c.inc(ins, "tXF")
            c.wait("vector", "tXF", c.val("tXF"))
            ins = c.v.tensor_copy(xs_fm[:, :, su * 128 : (su + 1) * 128], pbt[bk][:, :, :])
            c.inc(ins, "vXF")

        c.v.memset(y_acc[:], 0.0)

        for e in range(9):
            for iq in range(4):
                pidx = len(piece_used)
                pbuf = pidx % 2
                if pidx >= 2:
                    c.wait("gpsimd", "tDN", piece_used[pidx - 2])
                if e < 8:
                    gsrc = t["rt_gate"][e, :, iq * 512 : (iq + 1) * 512]
                    usrc = t["rt_up"][e, :, iq * 512 : (iq + 1) * 512]
                    dsrc = t["rt_down"][e, iq * 512 : (iq + 1) * 512, :]
                else:
                    gsrc = t["sh_gate"][:, iq * 512 : (iq + 1) * 512]
                    usrc = t["sh_up"][:, iq * 512 : (iq + 1) * 512]
                    dsrc = t["sh_down"][iq * 512 : (iq + 1) * 512, :]
                c.inc(c.g.dma_start(out=gate_p[:, pbuf], in_=gsrc.rearrange("(fc p) i -> p fc i", p=128)), "dW", 16)
                c.inc(c.g.dma_start(out=up_p[:, pbuf], in_=usrc.rearrange("(fc p) i -> p fc i", p=128)), "dW", 16)
                c.inc(c.g.dma_start(out=down_p[:, pbuf], in_=dsrc.rearrange("(ic p) h -> p ic h", p=128)), "dW", 16)
                dw_here = c.val("dW")
                xin = x_fm if e == 8 else xs_fm
                for tt in range(2):
                    for ic in range(4):
                        gk = (ic % 2) * 2
                        uk = gk + 1
                        c.wait("tensor", "dW", dw_here - 16)
                        if True:
                            c.wait("tensor", "vAC", c.val("vAC") - 1 if (c.val("vAC") >= 2) else 0)
                        ins = None
                        for fc in range(8):
                            ins = c.te.matmul(pb[gk], gate_p[:, pbuf, fc, ic * 128 : (ic + 1) * 128],
                                              xin[:, fc, tt * 512 : (tt + 1) * 512],
                                              start=(fc == 0), stop=(fc == 7))
                        c.inc(ins, "tGU")
                        ins = None
                        for fc in range(8):
                            ins = c.te.matmul(pb[uk], up_p[:, pbuf, fc, ic * 128 : (ic + 1) * 128],
                                              xin[:, fc, tt * 512 : (tt + 1) * 512],
                                              start=(fc == 0), stop=(fc == 7))
                        c.inc(ins, "tGU")
                        c.wait("scalar", "tGU", c.val("tGU") - 1)
                        c.wait("scalar", "vAC", c.val("vAC"))
                        ins = c.s.activation(sg[:], pb[gk], AF.Silu)
                        c.inc(ins, "sSI")
                        c.wait("vector", "tGU", c.val("tGU"))
                        c.wait("vector", "sSI", c.val("sSI"))
                        ins = c.v.tensor_tensor(act_q[:, tt, ic, :], sg[:], pb[uk], ALU.mult)
                        c.inc(ins, "vAC")
                    for s4 in range(4):
                        su = tt * 4 + s4
                        dbank = 4 if (su % 2 == 0) else 6
                        c.wait("tensor", "dW", dw_here)
                        c.wait("tensor", "vAC", c.val("vAC"))
                        if c.val("tDN") >= 2:
                            c.wait("tensor", "vYA", c.val("vYA") - 1)
                        ins = None
                        for ic in range(4):
                            for nh in range(2):
                                ins = c.te.matmul(pb[dbank + nh], act_q[:, tt, ic, s4 * 128 : (s4 + 1) * 128],
                                                  down_p[:, pbuf, ic, nh * 512 : (nh + 1) * 512],
                                                  start=(ic == 0), stop=(ic == 3))
                        c.inc(ins, "tDN")
                        c.wait("vector", "tDN", c.val("tDN"))
                        psflat = t["pb_flat"][:, dbank * 512 : (dbank + 2) * 512]
                        if e < 8:
                            ins = c.v.scalar_tensor_tensor(y_acc[:, su, :], psflat, masks8[:, su, e : e + 1],
                                                           y_acc[:, su, :], ALU.mult, ALU.add)
                        else:
                            ins = c.v.scalar_tensor_tensor(y_acc[:, su, :], psflat, 1.0,
                                                           y_acc[:, su, :], ALU.mult, ALU.add)
                        c.inc(ins, "vYA")
                piece_used.append(c.val("tDN"))

        for su in range(8):
            ti = tiles[su]
            ins = c.v.tensor_tensor(y_acc[:, su, :], t["stream"][:, ti, :], y_acc[:, su, :], ALU.add)
        c.v.drain()
        for su in range(8):
            c.v.reduce_sum(s1c[:, su : su + 1], y_acc[:, su, :], axis=AX.X)
            c.v.tensor_tensor(zsq[:], y_acc[:, su, :], y_acc[:, su, :], ALU.mult)
            c.v.drain()
            c.v.reduce_sum(s2c[:, su : su + 1], zsq[:], axis=AX.X)
        c.v.drain()
        c.v.tensor_scalar(mean8[:], s1c[:], 1.0 / 1024.0, None, ALU.mult)
        c.v.drain()
        c.v.tensor_tensor(msq8[:], mean8[:], mean8[:], ALU.mult)
        c.v.drain()
        c.v.scalar_tensor_tensor(var8[:], s2c[:], 1.0 / 1024.0, msq8[:], ALU.mult, ALU.subtract)
        c.v.drain()
        c.v.tensor_scalar(var8[:], var8[:], EPS, None, ALU.add)
        c.v.drain()
        c.inc(c.v.nop(), "vST")
        c.wait("scalar", "vST", c.val("vST"))
        c.s.activation(lnv8[:], var8[:], AF.Ln)
        c.s.drain()
        c.s.activation(rstd8[:], lnv8[:], AF.Exp, scale=-0.5)
        c.s.drain()
        c.inc(c.s.nop(), "sRS")
        c.wait("vector", "sRS", c.val("sRS"))
        for su in range(8):
            ti = tiles[su]
            ins = c.v.tensor_scalar(t["stream"][:, ti, :], y_acc[:, su, :],
                                    mean8[:, su : su + 1], rstd8[:, su : su + 1],
                                    ALU.subtract, ALU.mult)
            c.inc(ins, "vZ")



def emit_p3(c, t):
    a = Arena(t["arena"], t["arena_bytes"])
    wqkv = a.alloc([8, 3072], BF16)
    wout = a.alloc([8, 1024], BF16)
    x_fm = a.alloc([8, 256], BF16)
    raw = a.alloc([1024], F32)
    q_tm = a.alloc([2, 1024], BF16)
    k_tm = a.alloc([2, 1024], BF16)
    q64 = a.alloc([16, 256], BF16)
    k64 = a.alloc([16, 256], BF16)
    v_ext = a.alloc([2, 16, 65], BF16)
    atb = a.alloc([2, 2, 256], BF16)
    num_sb = a.alloc([2, 16, 65], F32)
    ov = a.alloc([2, 1024], BF16)
    o_fm = a.alloc([8, 256], BF16)
    s_sb = a.alloc([16, 65], F32)
    s_bf = a.alloc([16, 65], BF16)
    m0 = a.alloc([256], F32)
    m1 = a.alloc([256], F32)
    z1k = a.alloc([1024], F32)
    zsq = a.alloc([1024], F32)
    elu_t = zsq
    st4 = a.alloc([8, 2], F32)
    assert a.off <= t["arena_bytes"], a.off

    s1c = st4[:, 0, :]
    s2c = st4[:, 1, :]
    mean2 = st4[:, 2, :]
    msq2 = st4[:, 3, :]
    var2 = st4[:, 4, :]
    lnv2 = st4[:, 5, :]
    rstd2 = st4[:, 6, :]

    pb = t["pb"]
    pbt = [pb[i].bitcast(BF16).rearrange("p (a b) -> p a b", a=8) for i in range(2)]
    pp = [t["pb_flat"][:, 2 * 512 : 4 * 512], t["pb_flat"][:, 4 * 512 : 6 * 512]]
    p6 = pb[6].rearrange("p (a b) -> p a b", a=2)
    p7 = pb[7][:, 0 : 130].rearrange("p (a b) -> p a b", a=2)

    p2_vz = c.val("vZ")

    c.wait("gpsimd", "vZ", p2_vz)
    c.inc(c.g.dma_start(out=wqkv[:], in_=t["wqkvT"].rearrange("(fc p) o -> p fc o", p=128)), "dW", 16)
    c.inc(c.g.dma_start(out=wout[:], in_=t["woutT"].rearrange("(fc p) o -> p fc o", p=128)), "dW", 16)
    dw_p3 = c.val("dW")
    c.g.memset(m0[:], 1.0)
    ins = c.g.affine_select(out=m0[:], in_=m0[:], compare_op=ALU.is_ge, fill=0.0,
                            base=0, pattern=[[1, 256]], channel_multiplier=-1)
    c.inc(ins, "gI")
    c.g.memset(m1[:], 1.0)
    ins = c.g.affine_select(out=m1[:], in_=m1[:], compare_op=ALU.is_ge, fill=0.0,
                            base=-128, pattern=[[1, 256]], channel_multiplier=-1)
    c.inc(ins, "gI")
    gi_p3 = c.val("gI")

    conv_stage = a.alloc([2, 2048], BF16)
    convs = []
    for fc in range(8):
        for ob in range(2):
            convs.append((t["w1T"][fc * 128 : (fc + 1) * 128, ob * 2048 : (ob + 1) * 2048],
                          t["w1c"][:, fc, ob * 2048 : (ob + 1) * 2048]))
    for fc in range(8):
        for ob in range(2):
            convs.append((t["w2T"][fc * 128 : (fc + 1) * 128, ob * 2048 : (ob + 1) * 2048],
                          t["w2c"][:, fc, ob * 2048 : (ob + 1) * 2048]))
    for fc in range(32):
        for ob in range(2):
            convs.append((t["h1T"][fc * 128 : (fc + 1) * 128, ob * 2048 : (ob + 1) * 2048],
                          t["h1c"][:, fc, ob * 2048 : (ob + 1) * 2048]))
    for fc in range(32):
        for ob in range(2):
            convs.append((t["h2T"][fc * 128 : (fc + 1) * 128, ob * 2048 : (ob + 1) * 2048],
                          t["h2c"][:, fc, ob * 2048 : (ob + 1) * 2048]))
    for fc in range(32):
        convs.append((t["w3T"][fc * 128 : (fc + 1) * 128, :], t["w3c"][:, fc, :]))
    conv_marks = {}
    for i, (srcap, dstap) in enumerate(convs):
        n = srcap.shape[-1]
        if i >= 2:
            c.wait("gpsimd", "dCS", 16 * (i - 1))
        ins = c.g.dma_start(out=conv_stage[:, i % 2, 0 : n], in_=srcap)
        c.inc(ins, "dC", 16)
        c.wait("sync", "dC", 16 * (i + 1))
        ins = c.sy.dma_start(out=dstap, in_=conv_stage[:, i % 2, 0 : n])
        c.inc(ins, "dCS", 16)
        if i == 31:
            conv_marks["w1w2"] = c.val("dCS")
        if i == 95:
            conv_marks["h1"] = c.val("dCS")
        if i == 159:
            conv_marks["h2"] = c.val("dCS")
    conv_marks["w3"] = c.val("dCS")
    t["conv_marks"] = conv_marks

    c.v.memset(s_sb[:], 0.0)
    c.v.memset(s_bf[:], 0.0)
    c.v.memset(v_ext[:, :, :, 64 : 65], 1.0)
    c.v.drain()
    c.inc(c.v.nop(), "vSU")
    c.inc(c.v.nop(), "vEL")

    for cc in range(16):
        ti0 = 2 * cc
        for sub in range(2):
            g = cc * 2 + sub
            bk = g % 2
            if g == 0:
                c.wait("tensor", "vZ", p2_vz)
                c.wait("tensor", "vEL", c.val("vEL"))
                c.wait("tensor", "vSU", c.val("vSU"))
                c.wait("tensor", "gI", gi_p3)
            else:
                c.wait("tensor", "vXF", c.val("vXF"))
                if cc > 0:
                    c.wait("tensor", "vOT", c.val("vOT"))
            ins = None
            for fc in range(8):
                ins = c.te.transpose(pbt[bk][:, fc, :], t["stream"][:, ti0 + sub, fc * 128 : (fc + 1) * 128], t["ident"][:])
            c.inc(ins, "tXF")
            c.wait("vector", "tXF", c.val("tXF"))
            ins = c.v.tensor_copy(x_fm[:, :, sub * 128 : (sub + 1) * 128], pbt[bk][:, :, :])
            c.inc(ins, "vXF")

        c.wait("tensor", "vXF", c.val("vXF"))
        if cc == 0:
            c.wait("tensor", "dW", dw_p3)
        for sub in range(2):
            for which in range(3):
                j = (cc * 6 + sub * 3 + which) % 2
                base = which * 1024
                c.wait("tensor", "sQE", c.val("sQE"))
                c.wait("tensor", "vEL", c.val("vEL"))
                c.wait("tensor", "vPS", c.val("vPS"))
                ins = None
                for fc in range(8):
                    for nh in range(2):
                        ins = c.te.matmul(pp[j][:, nh * 512 : (nh + 1) * 512],
                                          x_fm[:, fc, sub * 128 : (sub + 1) * 128],
                                          wqkv[:, fc, base + nh * 512 : base + (nh + 1) * 512],
                                          start=(fc == 0), stop=(fc == 7))
                c.inc(ins, "tQKV")
                if which == 2:
                    c.wait("vector", "tQKV", c.val("tQKV"))
                    if cc > 0:
                        c.wait("vector", "tSU", c.val("tSU"))
                    ins = c.v.tensor_copy(v_ext[:, sub, :, 0 : 64],
                                          pp[j].rearrange("p (h d) -> p h d", d=64))
                    c.inc(ins, "vEL")
                else:
                    c.wait("scalar", "tQKV", c.val("tQKV"))
                    c.wait("scalar", "vEL", c.val("vEL"))
                    ins = c.s.copy(raw[:], pp[j])
                    c.inc(ins, "sQE")
                    c.wait("vector", "sQE", c.val("sQE"))
                    ins = c.v.tensor_scalar(elu_t[:], raw[:], 0.0, None, ALU.min)
                    c.inc(ins, "vEL")
                    c.wait("scalar", "vEL", c.val("vEL"))
                    ins = c.s.activation(elu_t[:], elu_t[:], AF.Exp)
                    c.inc(ins, "sQE")
                    if which == 0:
                        ins = c.v.tensor_scalar(raw[:], raw[:], 0.125, 0.0, ALU.mult, ALU.max)
                        c.inc(ins, "vEL")
                        c.v.drain()
                        c.wait("vector", "sQE", c.val("sQE"))
                        ins = c.v.scalar_tensor_tensor(q_tm[:, sub, :], elu_t[:], 0.125, raw[:], ALU.mult, ALU.add)
                        c.inc(ins, "vEL")
                    else:
                        ins = c.v.tensor_scalar(raw[:], raw[:], 0.0, None, ALU.max)
                        c.inc(ins, "vEL")
                        c.v.drain()
                        c.wait("vector", "sQE", c.val("sQE"))
                        ins = c.v.tensor_tensor(k_tm[:, sub, :], elu_t[:], raw[:], ALU.add)
                        c.inc(ins, "vEL")

        c.wait("tensor", "vEL", c.val("vEL"))
        tgt = 0
        for src_tm, dst64 in ((q_tm, q64), (k_tm, k64)):
            for sub in range(2):
                for hg in range(2):
                    g = cc * 8 + tgt
                    tgt += 1
                    bk = g % 2
                    c.wait("tensor", "vKT", c.val("vKT") - 1 if c.val("vKT") >= 2 else 0)
                    ins = None
                    for j in range(8):
                        h = hg * 8 + j
                        ins = c.te.transpose(pb[bk].bitcast(BF16)[0 : 64, j * 128 : (j + 1) * 128],
                                             src_tm[:, sub, h * 64 : (h + 1) * 64], t["ident"][:])
                    c.inc(ins, "tKT")
                    c.wait("vector", "tKT", c.val("tKT"))
                    ins = c.v.tensor_copy(dst64[0 : 64, hg * 8 : (hg + 1) * 8, sub * 128 : (sub + 1) * 128],
                                          pb[bk].bitcast(BF16)[0 : 64, :].rearrange("p (a b) -> p a b", a=8))
                    c.inc(ins, "vKT")

        c.wait("tensor", "vKT", c.val("vKT"))
        pa = [pb[i].rearrange("p (a b) -> p a b", a=2) for i in range(4)]
        pn = [pb[6][:, 0 : 130].rearrange("p (a b) -> p a b", a=2),
              pb[7][:, 0 : 130].rearrange("p (a b) -> p a b", a=2)]
        for h in range(16):
            ab = h % 4
            if h >= 4 or cc > 0:
                c.wait("tensor", "vMA", c.val("vMA") - 3 if c.val("vMA") >= 4 else 0)
            ins = None
            for mhalf in range(2):
                ins = c.te.matmul(pa[ab][:, mhalf, :], k64[0 : 64, h, mhalf * 128 : (mhalf + 1) * 128],
                                  q64[0 : 64, h, :], start=True, stop=True)
            c.inc(ins, "tAT")
            c.wait("vector", "tAT", c.val("tAT"))
            if h >= 2 or cc > 0:
                c.wait("vector", "tNU", c.val("tNU") - 1 if c.val("tNU") >= 2 else 0)
            ins = None
            for mhalf, mk_ in ((0, m0), (1, m1)):
                ins = c.v.tensor_tensor(atb[:, h % 2, mhalf, :], pa[ab][:, mhalf, :], mk_[:], ALU.mult)
            c.inc(ins, "vMA")
            c.wait("tensor", "vMA", c.val("vMA"))
            if h >= 2 or cc > 0:
                c.wait("tensor", "sNU", c.val("sNU") - 1 if c.val("sNU") >= 2 else 0)
            ins = None
            for lhalf in range(2):
                sl = slice(lhalf * 128, (lhalf + 1) * 128)
                c.te.matmul(pn[h % 2][:, lhalf, :], atb[:, h % 2, 0, sl], v_ext[:, 0, h, :], start=True, stop=False)
                c.te.matmul(pn[h % 2][:, lhalf, :], atb[:, h % 2, 1, sl], v_ext[:, 1, h, :], start=False, stop=False)
                ins = c.te.matmul(pn[h % 2][:, lhalf, :], q64[0 : 64, h, sl], s_bf[0 : 64, h, :], start=False, stop=True)
            c.inc(ins, "tNU")
            c.wait("scalar", "tNU", c.val("tNU"))
            ins = c.s.copy(num_sb[:, :, h, :], pn[h % 2][:, :, :])
            c.inc(ins, "sNU")

        c.wait("tensor", "vMA", c.val("vMA"))
        for hb in range(4):
            pS = pb[2 + hb][:, 0 : 260].rearrange("p (a b) -> p a b", a=4)
            ins = None
            for hh in range(4):
                h = hb * 4 + hh
                c.te.matmul(pS[0 : 64, hh, :], k_tm[:, 0, h * 64 : (h + 1) * 64], v_ext[:, 0, h, :],
                            start=True, stop=False)
                ins = c.te.matmul(pS[0 : 64, hh, :], k_tm[:, 1, h * 64 : (h + 1) * 64], v_ext[:, 1, h, :],
                                  start=False, stop=True)
            c.inc(ins, "tSU")
            c.wait("vector", "tSU", c.val("tSU"))
            ins = c.v.tensor_tensor(s_sb[0 : 64, hb * 4 : (hb + 1) * 4, :],
                                    s_sb[0 : 64, hb * 4 : (hb + 1) * 4, :], pS[0 : 64, :, :], ALU.add)
            c.v.drain()
            ins = c.v.tensor_copy(s_bf[0 : 64, hb * 4 : (hb + 1) * 4, :],
                                  s_sb[0 : 64, hb * 4 : (hb + 1) * 4, :])
            c.inc(ins, "vSU")

        c.wait("vector", "sNU", c.val("sNU"))
        ins = c.v.tensor_scalar(num_sb[:, :, :, 64 : 65], num_sb[:, :, :, 64 : 65], 1e-6, None, ALU.max)
        c.v.drain()
        ins = c.v.reciprocal(num_sb[:, :, :, 64 : 65], num_sb[:, :, :, 64 : 65])
        c.v.drain()
        rdb = num_sb[:, :, :, 64 : 65].broadcast_to([128, 2, 16, 64])
        ins = c.v.tensor_tensor(ov[:].rearrange("p a (h d) -> p a h d", d=64),
                                num_sb[:, :, :, 0 : 64], rdb, ALU.mult)
        c.inc(ins, "vDV")

        c.wait("tensor", "vDV", c.val("vDV"))
        for lhalf in range(2):
            g = cc * 2 + lhalf
            bk = g % 2
            if True:
                c.wait("tensor", "vOT", c.val("vOT") - 1 if c.val("vOT") >= 2 else 0)
            ins = None
            for fc in range(8):
                ins = c.te.transpose(pbt[bk][:, fc, :], ov[:, lhalf, fc * 128 : (fc + 1) * 128], t["ident"][:])
            c.inc(ins, "tOT")
            c.wait("vector", "tOT", c.val("tOT"))
            ins = c.v.tensor_copy(o_fm[:, :, lhalf * 128 : (lhalf + 1) * 128], pbt[bk][:, :, :])
            c.inc(ins, "vOT")

        c.wait("tensor", "vOT", c.val("vOT"))
        c.wait("tensor", "vSU", c.val("vSU"))
        for sub in range(2):
            j = sub
            ins = None
            for fc in range(8):
                for nh in range(2):
                    ins = c.te.matmul(pp[j][:, nh * 512 : (nh + 1) * 512],
                                      o_fm[:, fc, sub * 128 : (sub + 1) * 128],
                                      wout[:, fc, nh * 512 : (nh + 1) * 512],
                                      start=(fc == 0), stop=(fc == 7))
            c.inc(ins, "tWO")
        c.wait("vector", "tWO", c.val("tWO"))
        for sub in range(2):
            ins = c.v.tensor_tensor(z1k[:], t["stream"][:, ti0 + sub, :], pp[sub], ALU.add)
            c.inc(ins, "vPS")
            c.v.drain()
            c.v.reduce_sum(s1c[:, sub : sub + 1], z1k[:], axis=AX.X)
            c.v.tensor_tensor(zsq[:], z1k[:], z1k[:], ALU.mult)
            c.v.drain()
            c.v.reduce_sum(s2c[:, sub : sub + 1], zsq[:], axis=AX.X)
            c.v.drain()
            c.v.tensor_scalar(mean2[:, sub : sub + 1], s1c[:, sub : sub + 1], 1.0 / 1024.0, None, ALU.mult)
            c.v.drain()
            c.v.tensor_tensor(msq2[:, sub : sub + 1], mean2[:, sub : sub + 1], mean2[:, sub : sub + 1], ALU.mult)
            c.v.drain()
            c.v.scalar_tensor_tensor(var2[:, sub : sub + 1], s2c[:, sub : sub + 1], 1.0 / 1024.0,
                                     msq2[:, sub : sub + 1], ALU.mult, ALU.subtract)
            c.v.drain()
            c.v.tensor_scalar(var2[:, sub : sub + 1], var2[:, sub : sub + 1], EPS, None, ALU.add)
            c.v.drain()
            c.inc(c.v.nop(), "vST")
            c.wait("scalar", "vST", c.val("vST"))
            c.s.activation(lnv2[:, sub : sub + 1], var2[:, sub : sub + 1], AF.Ln)
            c.s.drain()
            c.s.activation(rstd2[:, sub : sub + 1], lnv2[:, sub : sub + 1], AF.Exp, scale=-0.5)
            c.s.drain()
            c.inc(c.s.nop(), "sRS")
            c.wait("vector", "sRS", c.val("sRS"))
            ins = c.v.tensor_scalar(t["stream"][:, ti0 + sub, :], z1k[:],
                                    mean2[:, sub : sub + 1], rstd2[:, sub : sub + 1],
                                    ALU.subtract, ALU.mult)
            c.inc(ins, "vZ")



def emit_p4(c, t):
    a = Arena(t["arena"], t["arena_bytes"])
    off0 = a.off
    x_fm = a.alloc([8, 512], BF16)
    w1p = a.alloc([2, 8, 256], BF16)
    w2p = a.alloc([2, 8, 256], BF16)
    sgl = a.alloc([512], F32)
    h1p = a.alloc([2, 32, 128], BF16)
    w3p = _carve(t["arena"], off0, [2, 32, 256], BF16)
    h_fm = a.alloc([32, 512], BF16)
    hh_fm = a.alloc([32, 512], BF16)
    scol = a.alloc([4, 32], F32)
    s3b = a.alloc([1024], F32)
    z2 = a.alloc([2, 1024], F32)
    zsq = a.alloc([1024], F32)
    st4 = a.alloc([8, 4], F32)
    assert a.off <= t["arena_bytes"], a.off
    s3r = z2[0:1, 0, :]
    onec = z2[0:1, 1, 0:128]

    s1c = st4[:, 0, :]
    s2c = st4[:, 1, :]
    mean4 = st4[:, 2, :]
    msq4 = st4[:, 3, :]
    var4 = st4[:, 4, :]
    lnv4 = st4[:, 5, :]
    rstd4 = st4[:, 6, :]

    pb = t["pb"]
    pbt = [pb[i].bitcast(BF16).rearrange("p (a b) -> p a b", a=8) for i in range(2)]
    cm = t["conv_marks"]
    p3_vz = c.val("vZ")

    c.wait("sync", "vZ", p3_vz)
    for i, nm in enumerate(("s1c", "s2c", "sh1c", "sh2c")):
        c.inc(c.sy.dma_start(out=scol[:, i, :], in_=t[nm]), "dX", 16)
    c.inc(c.sy.dma_start(out=s3r, in_=t["s3row"]), "dX", 16)
    dx_p4 = c.val("dX")
    c.wait("vector", "dX", dx_p4)
    c.v.memset(onec, 1.0)
    c.v.drain()
    c.inc(c.v.nop(), "vH")
    c.wait("tensor", "vZ", p3_vz)
    c.wait("tensor", "vH", c.val("vH"))
    c.wait("tensor", "dX", dx_p4)
    ins = None
    for nh in range(2):
        ins = c.te.matmul(t["pb_flat"][:, (6 + nh) * 512 : (7 + nh) * 512], onec,
                          s3r[:, nh * 512 : (nh + 1) * 512], start=True, stop=True)
    c.inc(ins, "tH")
    c.wait("vector", "tH", c.val("tH"))
    ins = c.v.tensor_copy(s3b[:], t["pb_flat"][:, 6 * 512 : 8 * 512])
    c.v.drain()
    c.inc(ins, "vH")
    c.inc(c.v.nop(), "vPS")

    wu = []
    hpieces = []
    w3u = []
    out_dmas = []

    for tt in range(8):
        ti0 = tt * 4
        for g4 in range(4):
            bk = g4 % 2
            c.wait("tensor", "vH", c.val("vH"))
            if tt > 0:
                c.wait("tensor", "vPS", c.val("vPS"))
            ins = None
            for fc in range(8):
                ins = c.te.transpose(pbt[bk][:, fc, :], t["stream"][:, ti0 + g4, fc * 128 : (fc + 1) * 128], t["ident"][:])
            c.inc(ins, "tH")
            c.wait("vector", "tH", c.val("tH"))
            ins = c.v.tensor_copy(x_fm[:, :, g4 * 128 : (g4 + 1) * 128], pbt[bk][:, :, :])
            c.inc(ins, "vH")

        for ob in range(16):
            pidx = len(wu)
            buf = pidx % 2
            if pidx % 16 < 2:
                c.wait("gpsimd", "tH", c.val("tH"))
            elif pidx >= 2:
                c.wait("gpsimd", "tH", wu[pidx - 2])
            if pidx == 0:
                c.wait("gpsimd", "dCS", cm["w1w2"])
            c.inc(c.g.dma_start(out=w1p[:, buf], in_=t["w1c"][:, :, ob * 256 : (ob + 1) * 256]), "dWH", 16)
            c.inc(c.g.dma_start(out=w2p[:, buf], in_=t["w2c"][:, :, ob * 256 : (ob + 1) * 256]), "dWH", 16)
            c.wait("tensor", "dWH", c.val("dWH"))
            if ob == 0:
                c.wait("tensor", "vH", c.val("vH"))
            for oc in range(2):
                og = ob * 2 + oc
                gk = (oc % 2) * 2
                uk = gk + 1
                c.wait("tensor", "vH", c.val("vH") - 1 if (tt + og > 0) else 0)
                ins = None
                for fc in range(8):
                    ins = c.te.matmul(pb[gk], w1p[:, buf, fc, oc * 128 : (oc + 1) * 128], x_fm[:, fc, :],
                                      start=(fc == 0), stop=(fc == 7))
                c.inc(ins, "tH")
                ins = None
                for fc in range(8):
                    ins = c.te.matmul(pb[uk], w2p[:, buf, fc, oc * 128 : (oc + 1) * 128], x_fm[:, fc, :],
                                      start=(fc == 0), stop=(fc == 7))
                c.inc(ins, "tH")
                c.wait("scalar", "tH", c.val("tH") - 1)
                c.wait("scalar", "vH", c.val("vH"))
                ins = c.s.activation(sgl[:], pb[gk], AF.Silu, scale=scol[:, 0, og : og + 1])
                c.inc(ins, "sH")
                c.wait("vector", "tH", c.val("tH"))
                c.wait("vector", "sH", c.val("sH"))
                ins = c.v.scalar_tensor_tensor(h_fm[:, og, :], pb[uk], scol[:, 1, og : og + 1], sgl[:],
                                               ALU.mult, ALU.mult)
                c.inc(ins, "vH")

        for phase in range(2):
            srcbuf = h_fm if phase == 0 else hh_fm
            dstbuf = hh_fm if phase == 0 else h_fm
            cache = t["h1c"] if phase == 0 else t["h2c"]
            scidx = 2 + phase
            for ob in range(32):
                pidx = len(hpieces)
                buf = pidx % 2
                if pidx >= 2:
                    c.wait("sync", "tH", hpieces[pidx - 2])
                if tt == 0 and phase == 0 and ob == 0:
                    c.wait("sync", "dCS", cm["h1"])
                if tt == 0 and phase == 1 and ob == 0:
                    c.wait("sync", "dCS", cm["h2"])
                c.inc(c.sy.dma_start(out=h1p[:, buf], in_=cache[:, :, ob * 128 : (ob + 1) * 128]), "dK", 16)
                c.wait("tensor", "dK", c.val("dK"))
                if ob == 0:
                    c.wait("tensor", "vH", c.val("vH"))
                    c.wait("tensor", "sH", c.val("sH"))
                og = ob
                bk45 = 4 + (og % 2)
                if tt + phase + og > 0:
                    c.wait("tensor", "sH", c.val("sH") - 1)
                ins = None
                for ic in range(32):
                    ins = c.te.matmul(pb[bk45], h1p[:, buf, ic, :], srcbuf[:, ic, :],
                                      start=(ic == 0), stop=(ic == 31))
                c.inc(ins, "tH")
                hpieces.append(c.val("tH"))
                c.wait("scalar", "tH", c.val("tH"))
                ins = c.s.activation(dstbuf[:, og, :], pb[bk45], AF.Gelu, scale=scol[:, scidx, og : og + 1])
                c.inc(ins, "sH")

        for qtr in range(4):
            pidx = len(w3u)
            buf = pidx % 2
            if pidx % 4 < 2:
                c.wait("sync", "tH", c.val("tH"))
            elif pidx >= 2:
                c.wait("sync", "tH", w3u[pidx - 2])
            if tt == 0 and qtr == 0:
                c.wait("sync", "dCS", cm["w3"])
            c.inc(c.sy.dma_start(out=w3p[:, buf], in_=t["w3c"][:, :, qtr * 256 : (qtr + 1) * 256]), "dK", 16)
            c.wait("tensor", "dK", c.val("dK"))
            if qtr == 0:
                c.wait("tensor", "sH", c.val("sH"))
                c.wait("tensor", "vPS", c.val("vPS"))
            for sub in range(4):
                ps_q = t["pb_flat"][:, (2 * sub) * 512 + qtr * 256 : (2 * sub) * 512 + (qtr + 1) * 256]
                ins = None
                for ic in range(32):
                    ins = c.te.matmul(ps_q, h_fm[:, ic, sub * 128 : (sub + 1) * 128],
                                      w3p[:, buf, ic, :], start=(ic == 0), stop=(ic == 31))
                c.inc(ins, "tH")
            w3u.append(c.val("tH"))
        for pair in range(2):
            for k in range(2):
                sub = pair * 2 + k
                ti = ti0 + sub
                psflat = t["pb_flat"][:, (2 * sub) * 512 : (2 * sub + 2) * 512]
                c.wait("vector", "tH", c.val("tH"))
                if len(out_dmas) >= 2:
                    c.wait("vector", "dO", out_dmas[-2])
                ins = c.v.tensor_tensor(z2[:, k, :], psflat, s3b[:], ALU.mult)
                c.inc(ins, "vPS")
                c.v.drain()
                ins = c.v.tensor_tensor(z2[:, k, :], z2[:, k, :], t["stream"][:, ti, :], ALU.add)
                c.v.drain()
                c.v.reduce_sum(s1c[:, sub : sub + 1], z2[:, k, :], axis=AX.X)
                c.v.tensor_tensor(zsq[:], z2[:, k, :], z2[:, k, :], ALU.mult)
                c.v.drain()
                c.v.reduce_sum(s2c[:, sub : sub + 1], zsq[:], axis=AX.X)
            c.v.drain()
            sl = slice(pair * 2, pair * 2 + 2)
            c.v.tensor_scalar(mean4[:, sl], s1c[:, sl], 1.0 / 1024.0, None, ALU.mult)
            c.v.drain()
            c.v.tensor_tensor(msq4[:, sl], mean4[:, sl], mean4[:, sl], ALU.mult)
            c.v.drain()
            c.v.scalar_tensor_tensor(var4[:, sl], s2c[:, sl], 1.0 / 1024.0, msq4[:, sl], ALU.mult, ALU.subtract)
            c.v.drain()
            c.v.tensor_scalar(var4[:, sl], var4[:, sl], EPS, None, ALU.add)
            c.v.drain()
            c.inc(c.v.nop(), "vST")
            c.wait("scalar", "vST", c.val("vST"))
            c.s.activation(lnv4[:, sl], var4[:, sl], AF.Ln)
            c.s.drain()
            c.s.activation(rstd4[:, sl], lnv4[:, sl], AF.Exp, scale=-0.5)
            c.s.drain()
            c.inc(c.s.nop(), "sRS")
            c.wait("vector", "sRS", c.val("sRS"))
            for k in range(2):
                sub = pair * 2 + k
                ti = ti0 + sub
                ins = c.v.tensor_scalar(z2[:, k, :], z2[:, k, :],
                                        mean4[:, sub : sub + 1], rstd4[:, sub : sub + 1],
                                        ALU.subtract, ALU.mult)
                c.v.drain()
                c.inc(c.v.nop(), "vZ")
                c.wait("scalar", "vZ", c.val("vZ"))
                ins = c.s.dma_start(out=t["out"][ti * 128 : (ti + 1) * 128, :], in_=z2[:, k, :])
                out_dmas.append(c.inc(ins, "dO", 16))


ARENA_BYTES = 146944


def build_nc(upto=4):
    nc = bass.Bass("TRN2", target_bir_lowering=False)
    t = {}
    t["arena_bytes"] = ARENA_BYTES

    t["x"] = nc.dram_tensor("x", [T, D], F32, kind="ExternalInput").ap()
    t["wqT"] = nc.dram_tensor("wqT", [D, D], F32, kind="ExternalInput").ap()
    t["wkT"] = nc.dram_tensor("wkT", [D, D], F32, kind="ExternalInput").ap()
    t["wvT"] = nc.dram_tensor("wvT", [D, D], F32, kind="ExternalInput").ap()
    t["out"] = nc.dram_tensor("out", [T, D], F32, kind="ExternalOutput").ap()
    t["gate_w"] = nc.dram_tensor("gate_w", [D, 8], F32, kind="ExternalInput").ap()
    t["sh_gate"] = nc.dram_tensor("sh_gate", [D, 2048], F32, kind="ExternalInput").ap()
    t["sh_up"] = nc.dram_tensor("sh_up", [D, 2048], F32, kind="ExternalInput").ap()
    t["sh_down"] = nc.dram_tensor("sh_down", [2048, D], F32, kind="ExternalInput").ap()
    t["rt_gate"] = nc.dram_tensor("rt_gate", [8, D, 2048], F32, kind="ExternalInput").ap()
    t["rt_up"] = nc.dram_tensor("rt_up", [8, D, 2048], F32, kind="ExternalInput").ap()
    t["rt_down"] = nc.dram_tensor("rt_down", [8, 2048, D], F32, kind="ExternalInput").ap()
    t["wqkvT"] = nc.dram_tensor("wqkvT", [D, 3072], F32, kind="ExternalInput").ap()
    t["w1T"] = nc.dram_tensor("w1T", [D, 4096], F32, kind="ExternalInput").ap()
    t["w2T"] = nc.dram_tensor("w2T", [D, 4096], F32, kind="ExternalInput").ap()
    t["h1T"] = nc.dram_tensor("h1T", [4096, 4096], F32, kind="ExternalInput").ap()
    t["h2T"] = nc.dram_tensor("h2T", [4096, 4096], F32, kind="ExternalInput").ap()
    t["w3T"] = nc.dram_tensor("w3T", [4096, D], F32, kind="ExternalInput").ap()
    t["s1c"] = nc.dram_tensor("s1c", [128, 32], F32, kind="ExternalInput").ap()
    t["s2c"] = nc.dram_tensor("s2c", [128, 32], F32, kind="ExternalInput").ap()
    t["sh1c"] = nc.dram_tensor("sh1c", [128, 32], F32, kind="ExternalInput").ap()
    t["sh2c"] = nc.dram_tensor("sh2c", [128, 32], F32, kind="ExternalInput").ap()
    t["s3row"] = nc.dram_tensor("s3row", [1, D], F32, kind="ExternalInput").ap()
    t["w1c"] = nc.dram_tensor("w1c", [128, 8, 4096], BF16, kind="Internal").ap()
    t["w2c"] = nc.dram_tensor("w2c", [128, 8, 4096], BF16, kind="Internal").ap()
    t["h1c"] = nc.dram_tensor("h1c", [128, 32, 4096], BF16, kind="Internal").ap()
    t["h2c"] = nc.dram_tensor("h2c", [128, 32, 4096], BF16, kind="Internal").ap()
    t["w3c"] = nc.dram_tensor("w3c", [128, 32, 1024], BF16, kind="Internal").ap()
    t["woutT"] = nc.dram_tensor("woutT", [D, D], F32, kind="ExternalInput").ap()

    t["q_spill"] = nc.dram_tensor("q_spill", [8, 128, 8, 512], BF16, kind="Internal").ap()
    t["k_spill"] = nc.dram_tensor("k_spill", [8, 128, 8, 512], BF16, kind="Internal").ap()
    t["v_spill"] = nc.dram_tensor("v_spill", [8, 128, 4, 1024], BF16, kind="Internal").ap()

    import contextlib
    es = contextlib.ExitStack()
    with es:
        t["stream"] = es.enter_context(nc.sbuf_tensor("stream", [128, NT, 1024], BF16))
        t["ident"] = es.enter_context(nc.sbuf_tensor("ident", [128, 128], BF16))
        arena = es.enter_context(nc.sbuf_tensor("arena", [128, ARENA_BYTES // 2], BF16))
        t["arena"] = arena
        pbs = es.enter_context(nc.psum_tensor("pball", [128, 8, 512], F32))
        t["pb"] = [pbs[:, i, :] for i in range(8)]
        t["pb_flat"] = pbs[:].rearrange("p a b -> p (a b)")
        sems = {}
        for name in SEMS:
            sems[name] = es.enter_context(nc.semaphore(name))
        block = es.enter_context(nc.Block())

        def run_pass(eng_name):
            def f(eng):
                cx = Ctx(nc, sems, eng_name, eng)
                emit_program(cx, t, upto=upto)
            return f

        block.sync(run_pass("sync"))
        block.vector(run_pass("vector"))
        block.scalar(run_pass("scalar"))
        block.tensor(run_pass("tensor"))
        block.gpsimd(run_pass("gpsimd"))
    return nc


def host_inputs_per_core(core, inp):
    asc = np.ascontiguousarray
    return {
        "x": asc(inp["L0_attn_x"][core]),
        "wqT": asc(inp["L0_attn_Wq"].T),
        "wkT": asc(inp["L0_attn_Wk"].T),
        "wvT": asc(inp["L0_attn_Wv"].T),
        "gate_w": asc(inp["L0_ffn_gate_w"]),
        "sh_gate": asc(inp["L0_ffn_sh_gate"]),
        "sh_up": asc(inp["L0_ffn_sh_up"]),
        "sh_down": asc(inp["L0_ffn_sh_down"]),
        "rt_gate": asc(inp["L0_ffn_rt_gate"]),
        "rt_up": asc(inp["L0_ffn_rt_up"]),
        "rt_down": asc(inp["L0_ffn_rt_down"]),
        "wqkvT": asc(inp["L1_attn_W_qkv"].T),
        "woutT": asc(inp["L1_attn_W_out"].T),
        "w1T": asc(inp["L1_ffn_w1_w"].T),
        "w2T": asc(inp["L1_ffn_w2_w"].T),
        "h1T": asc(inp["L1_ffn_h1_w"].T),
        "h2T": asc(inp["L1_ffn_h2_w"].T),
        "w3T": asc(inp["L1_ffn_w3_w"].T),
        "s1c": asc(inp["L1_ffn_w1_s"].reshape(32, 128).T),
        "s2c": asc(inp["L1_ffn_w2_s"].reshape(32, 128).T),
        "sh1c": asc(inp["L1_ffn_h1_s"].reshape(32, 128).T),
        "sh2c": asc(inp["L1_ffn_h2_s"].reshape(32, 128).T),
        "s3row": asc(inp["L1_ffn_w3_s"].reshape(1, 1024)),
    }


def kernel(**inputs):
    inputs = {k: np.asarray(v) for k, v in inputs.items()}
    nc = build_nc(upto=4)
    in_maps = [host_inputs_per_core(cc, inputs) for cc in range(8)]
    res = run_bass_kernel_spmd(nc, in_maps, core_ids=list(range(8)))
    return np.stack([res.results[cc]["out"] for cc in range(8)], axis=0)


# version 04 (final)
# speedup vs baseline: 1.0272x; 1.0034x over previous
import math
import numpy as np
import concourse.bass as bass
import concourse.mybir as mybir
from concourse.bass_utils import run_bass_kernel_spmd

F32 = mybir.dt.float32
BF16 = mybir.dt.bfloat16
ALU = mybir.AluOpType
AX = mybir.AxisListType
AF = mybir.ActivationFunctionType

D = 1024
T = 4096
NT = 32
EPS = 1e-5

SEMS = [
    "dX", "dQ", "dK", "dV", "dW", "dSp", "dO",
    "gI",
    "vX", "vF", "sE", "tTr", "tQK",
    "tSC", "sSC", "vCM", "vB", "sEX", "vRD", "tTW", "vWT", "tPV", "vPS",
    "sRS", "vST", "vZ",
    "tG", "sG", "vMK", "vXS", "tXF", "vXF", "tGU", "sSI", "vAC", "tDN", "vYA",
    "tQKV", "sQE", "vEL", "tAT", "vMA", "tNU", "sNU", "vDV", "tKT", "vKT",
    "tSU", "vSU", "tOT", "vOT", "tWO",
    "tH", "sH", "vH", "dWH", "dC", "dCS",
]


class _Null:
    def __getattr__(self, k):
        def f(*a, **kw):
            return None
        return f


_NULL = _Null()


class Ctx:
    def __init__(self, nc, sems, eng_name, eng):
        self.nc = nc
        self.sems = sems
        self.e = eng_name
        self.eng = eng
        self.cnt = {k: 0 for k in sems}
        self.v = eng if eng_name == "vector" else _NULL
        self.s = eng if eng_name == "scalar" else _NULL
        self.te = eng if eng_name == "tensor" else _NULL
        self.g = eng if eng_name == "gpsimd" else _NULL
        self.sy = eng if eng_name == "sync" else _NULL

    def wait(self, engname, sem, val=None):
        if val is None:
            val = self.cnt[sem]
        if self.e == engname:
            self.eng.wait_ge(self.sems[sem], val)

    def inc(self, ins, sem, k=1):
        self.cnt[sem] += k
        if ins is not None:
            ins.then_inc(self.sems[sem], k)
        return self.cnt[sem]

    def val(self, sem):
        return self.cnt[sem]


def _carve(arena, byte_off, shape, dtype):
    nelem = int(np.prod(shape))
    esz = 4 if dtype == F32 else 2
    assert byte_off % 4 == 0
    start2 = byte_off // 2
    ap = arena[:, start2 : start2 + nelem * esz // 2]
    if dtype == F32:
        ap = ap.bitcast(F32)
    if len(shape) == 1:
        return ap
    spec = "p (" + " ".join(f"a{i}" for i in range(len(shape))) + ") -> p " + " ".join(
        f"a{i}" for i in range(len(shape))
    )
    return ap.rearrange(spec, **{f"a{i}": s for i, s in enumerate(shape)})


class Arena:
    def __init__(self, arena_ap, nbytes, off=0):
        self.ap = arena_ap
        self.nbytes = nbytes
        self.off = off

    def alloc(self, shape, dtype):
        esz = 4 if dtype == F32 else 2
        nelem = int(np.prod(shape))
        sz = nelem * esz
        off = (self.off + 3) // 4 * 4
        assert off + sz <= self.nbytes, f"arena overflow: {off}+{sz} > {self.nbytes}"
        v = _carve(self.ap, off, shape, dtype)
        self.off = off + sz
        return v


TILE_LAYOUT = "(s p) f -> p s f"


def emit_program(c, t, upto=4):
    emit_init(c, t)
    emit_p0(c, t)
    emit_p1(c, t)
    if upto >= 2:
        emit_p2(c, t)
    if upto >= 3:
        emit_p3(c, t)
    if upto >= 4:
        emit_p4(c, t)
    else:
        emit_stream_out(c, t)
    c.wait("scalar", "dO")


def emit_init(c, t):
    c.g.memset(t["ident"][:], 0.0)
    ins = c.g.affine_select(
        out=t["ident"][:], in_=t["ident"][:], compare_op=ALU.not_equal,
        fill=1.0, base=0, pattern=[[-1, 128]], channel_multiplier=1)
    c.inc(ins, "gI")



def emit_p0(c, t):
    a = Arena(t["arena"], t["arena_bytes"])
    wq = a.alloc([8, 1024], BF16)
    wk = a.alloc([8, 1024], BF16)
    wv = a.alloc([8, 1024], BF16)
    xt = a.alloc([2, 4, 1024], F32)
    xb = a.alloc([4, 1024], BF16)
    xfm = a.alloc([8, 512], BF16)
    qst = a.alloc([8, 512], BF16)
    kst = a.alloc([8, 512], BF16)
    vst = a.alloc([4, 1024], BF16)
    assert a.off <= t["arena_bytes"], a.off

    x = t["x"]
    pb = t["pb"]
    pbt = [pb[i].bitcast(BF16).rearrange("p (a b) -> p a b", a=8) for i in range(2)]

    for w_ap, dst in ((t["wqT"], wq), (t["wkT"], wk), (t["wvT"], wv)):
        src = w_ap.rearrange("(fc p) o -> p fc o", p=128)
        c.inc(c.g.dma_start(out=dst[:], in_=src), "dW", 16)

    xb_done = []
    tr_done = []
    st_done = []
    mm_groups = [0]

    for m in range(8):
        buf = m % 2
        if m >= 2:
            c.wait("sync", "vX", xb_done[m - 2])
        ins = c.sy.dma_start(out=xt[:, buf], in_=x[m * 512 : (m + 1) * 512, :].rearrange(TILE_LAYOUT, p=128))
        c.inc(ins, "dX", 16)

        c.wait("vector", "dX", 16 * (m + 1))
        if m >= 1:
            c.wait("vector", "tTr", tr_done[m - 1])
        ins = c.v.tensor_copy(xb[:], xt[:, buf])
        xb_done.append(c.inc(ins, "vX"))

        c.wait("tensor", "vX", xb_done[m])
        if m == 0:
            c.wait("tensor", "gI", 1)
        for g in range(4):
            bk = g % 2
            if m * 4 + g >= 2:
                c.wait("tensor", "vF", c.val("vF") - 1)
            ins = None
            for fc in range(8):
                ins = c.te.transpose(pbt[bk][:, fc, :], xb[:, g, fc * 128 : (fc + 1) * 128], t["ident"][:])
            c.inc(ins, "tTr")
            c.wait("vector", "tTr", c.val("tTr"))
            ins = c.v.tensor_copy(xfm[:, :, g * 128 : (g + 1) * 128], pbt[bk][:, :, :])
            c.inc(ins, "vF")
        tr_done.append(c.val("tTr"))

        c.wait("tensor", "vF", c.val("vF"))
        if m == 0:
            c.wait("tensor", "dW", 48)
        if m >= 1:
            c.wait("tensor", "dSp", st_done[m - 1])

        def mmgroup(emit_mms, evac_dst):
            i = mm_groups[0]
            mm_groups[0] += 1
            bank = 2 + (i % 4)
            if i >= 4:
                c.wait("tensor", "sE", i - 3)
            ins = emit_mms(pb[bank])
            c.inc(ins, "tQK")
            c.wait("scalar", "tQK", c.val("tQK"))
            ins = c.s.copy(evac_dst, pb[bank])
            c.inc(ins, "sE")

        for wsb, stage in ((wq, qst), (wk, kst)):
            for oc in range(8):
                def mk(ps, wsb=wsb, oc=oc):
                    ins = None
                    for fc in range(8):
                        ins = c.te.matmul(ps, wsb[:, fc, oc * 128 : (oc + 1) * 128], xfm[:, fc, :],
                                          start=(fc == 0), stop=(fc == 7))
                    return ins
                mmgroup(mk, stage[:, oc, :])
        for s in range(4):
            for half in range(2):
                def mk(ps, s=s, half=half):
                    ins = None
                    for fc in range(8):
                        ins = c.te.matmul(ps, xfm[:, fc, s * 128 : (s + 1) * 128],
                                          wv[:, fc, half * 512 : (half + 1) * 512],
                                          start=(fc == 0), stop=(fc == 7))
                    return ins
                mmgroup(mk, vst[:, s, half * 512 : (half + 1) * 512])

        c.s.drain()
        c.inc(c.s.dma_start(out=t["q_spill"][m], in_=qst[:]), "dSp", 16)
        c.inc(c.s.dma_start(out=t["k_spill"][m], in_=kst[:]), "dSp", 16)
        c.inc(c.s.dma_start(out=t["v_spill"][m], in_=vst[:]), "dSp", 16)
        st_done.append(c.val("dSp"))



def emit_p1(c, t):
    a = Arena(t["arena"], t["arena_bytes"])
    qbuf = a.alloc([8, 512], BF16)
    kch = a.alloc([2, 8, 512], BF16)
    vch = a.alloc([2, 4, 1024], BF16)
    scb = a.alloc([4, 8, 512], BF16)
    wT = a.alloc([32, 512], BF16)
    xres = a.alloc([4, 1024], F32)
    z2 = a.alloc([2, 1024], F32)
    zsq = a.alloc([1024], F32)
    cmax = a.alloc([4, 8], F32)
    st8 = a.alloc([12, 4], F32)
    assert a.off <= t["arena_bytes"], a.off

    rmneg = st8[:, 0, :]
    bias4 = st8[:, 1, :]
    den4 = st8[:, 2, :]
    rden4 = st8[:, 3, :]
    s1c = st8[:, 4, :]
    s2c = st8[:, 5, :]
    mean4 = st8[:, 6, :]
    msq4 = st8[:, 7, :]
    var4 = st8[:, 8, :]
    lnv4 = st8[:, 9, :]
    rstd4 = st8[:, 10, :]

    pb = t["pb"]
    pbt = [pb[i].bitcast(BF16).rearrange("p (a b) -> p a b", a=8) for i in range(2)]
    x = t["x"]

    p0_dsp = c.val("dSp")
    p0_dx = c.val("dX")
    qk_used = {}
    kch_used = {}
    vch_used = {}
    epi_done = []
    ps_rel = []

    for m in range(8):
        if m == 0:
            c.wait("sync", "dSp", p0_dsp)
        if m >= 1:
            c.wait("sync", "tSC", qk_used[m - 1])
        ins = c.sy.dma_start(out=qbuf[:], in_=t["q_spill"][m])
        c.inc(ins, "dQ", 16)
        if m >= 1:
            c.wait("sync", "vZ", epi_done[m - 1])
        ins = c.sy.dma_start(out=xres[:], in_=x[m * 512 : (m + 1) * 512, :].rearrange(TILE_LAYOUT, p=128))
        c.inc(ins, "dX", 16)

        for kc in range(8):
            kb = kc % 2
            gidx = m * 8 + kc
            if gidx >= 2:
                c.wait("sync", "tSC", kch_used[gidx - 2])
            ins = c.sy.dma_start(out=kch[:, kb], in_=t["k_spill"][kc])
            c.inc(ins, "dK", 16)
            c.wait("tensor", "dK", 16 * (gidx + 1))
            if kc == 0:
                c.wait("tensor", "dQ", 16 * (m + 1))
                if m >= 1:
                    c.wait("tensor", "vPS", ps_rel[m - 1] - 2)
            if kc == 1 and m >= 1:
                c.wait("tensor", "vPS", ps_rel[m - 1])
            for s in range(4):
                bank = (0 if kc % 2 == 0 else 4) + s
                if gidx >= 2:
                    c.wait("tensor", "sSC", c.val("sSC") - 7)
                ins = None
                for fc in range(8):
                    ins = c.te.matmul(pb[bank], qbuf[:, fc, s * 128 : (s + 1) * 128],
                                      kch[:, kb, fc, :], start=(fc == 0), stop=(fc == 7))
                c.inc(ins, "tSC")
                c.wait("scalar", "tSC", c.val("tSC"))
                ins = c.s.copy(scb[:, s, kc, :], pb[bank])
                c.inc(ins, "sSC")
                c.wait("vector", "sSC", c.val("sSC"))
                ins = c.v.reduce_max(cmax[:, s, kc : kc + 1], scb[:, s, kc, :], axis=AX.X)
                c.inc(ins, "vCM")
            kch_used[gidx] = c.val("tSC")
        qk_used[m] = c.val("tSC")

        c.wait("vector", "vCM", c.val("vCM"))
        for s in range(4):
            c.v.reduce_max(rmneg[:, s : s + 1], cmax[:, s, :], axis=AX.X, negate=True)
        c.v.drain()
        c.v.tensor_scalar(bias4[:], rmneg[:], 1.0 / 32.0, None, ALU.mult)
        c.v.drain()
        c.inc(c.v.nop(), "vB")

        c.wait("scalar", "vB", c.val("vB"))
        for s in range(4):
            ins = c.s.activation(scb[:, s].rearrange("p a b -> p (a b)"),
                                 scb[:, s].rearrange("p a b -> p (a b)"),
                                 AF.Exp, bias=bias4[:, s : s + 1], scale=1.0 / 32.0,
                                 accum_out=den4[:, s : s + 1])
            c.inc(ins, "sEX")

        c.wait("vector", "sEX", c.val("sEX"))
        c.v.reciprocal(rden4[:], den4[:])
        c.v.drain()
        c.inc(c.v.nop(), "vRD")

        sex_base = c.val("sEX") - 4
        for s in range(4):
            c.wait("tensor", "sEX", sex_base + s + 1)
            for kc in range(8):
                g = s * 8 + kc
                bk = g % 2
                if g >= 2:
                    c.wait("tensor", "vWT", c.val("vWT") - 1)
                ins = None
                for j in range(4):
                    ins = c.te.transpose(pbt[bk][:, j, :], scb[:, s, kc, j * 128 : (j + 1) * 128], t["ident"][:])
                c.inc(ins, "tTW")
                c.wait("vector", "tTW", c.val("tTW"))
                ins = c.v.tensor_copy(wT[:, kc * 4 : kc * 4 + 4, s * 128 : (s + 1) * 128], pbt[bk][:, 0:4, :])
                c.inc(ins, "vWT")

        c.wait("tensor", "vWT", c.val("vWT"))
        for vc in range(8):
            vb = vc % 2
            gidx = m * 8 + vc
            if gidx >= 2:
                c.wait("sync", "tPV", vch_used[gidx - 2])
            ins = c.sy.dma_start(out=vch[:, vb], in_=t["v_spill"][vc])
            c.inc(ins, "dV", 16)
            c.wait("tensor", "dV", 16 * (gidx + 1))
            ins = None
            for s in range(4):
                for j in range(4):
                    kcg = vc * 4 + j
                    for half in range(2):
                        ins = c.te.matmul(pb[2 * s + half],
                                          wT[:, kcg, s * 128 : (s + 1) * 128],
                                          vch[:, vb, j, half * 512 : (half + 1) * 512],
                                          start=(kcg == 0), stop=(kcg == 31))
            c.inc(ins, "tPV")
            vch_used[gidx] = c.val("tPV")

        c.wait("vector", "dX", p0_dx + 16 * (m + 1))
        c.wait("vector", "vRD", c.val("vRD"))
        c.wait("vector", "tPV", c.val("tPV"))
        for pair in range(2):
            for k in range(2):
                s = pair * 2 + k
                psflat = t["pb_flat"][:, (2 * s) * 512 : (2 * s + 2) * 512]
                ins = c.v.scalar_tensor_tensor(z2[:, k, :], psflat, rden4[:, s : s + 1],
                                               xres[:, s, :], ALU.mult, ALU.add)
                c.inc(ins, "vPS")
                c.v.drain()
                c.v.reduce_sum(s1c[:, s : s + 1], z2[:, k, :], axis=AX.X)
                c.v.tensor_tensor(zsq[:], z2[:, k, :], z2[:, k, :], ALU.mult)
                c.v.drain()
                c.v.reduce_sum(s2c[:, s : s + 1], zsq[:], axis=AX.X)
            c.v.drain()
            sl = slice(pair * 2, pair * 2 + 2)
            c.v.tensor_scalar(mean4[:, sl], s1c[:, sl], 1.0 / 1024.0, None, ALU.mult)
            c.v.drain()
            c.v.tensor_tensor(msq4[:, sl], mean4[:, sl], mean4[:, sl], ALU.mult)
            c.v.drain()
            c.v.scalar_tensor_tensor(var4[:, sl], s2c[:, sl], 1.0 / 1024.0, msq4[:, sl],
                                     ALU.mult, ALU.subtract)
            c.v.drain()
            c.v.tensor_scalar(var4[:, sl], var4[:, sl], EPS, None, ALU.add)
            c.v.drain()
            c.inc(c.v.nop(), "vST")
            c.wait("scalar", "vST", c.val("vST"))
            c.s.activation(lnv4[:, sl], var4[:, sl], AF.Ln)
            c.s.drain()
            c.s.activation(rstd4[:, sl], lnv4[:, sl], AF.Exp, scale=-0.5)
            c.s.drain()
            c.inc(c.s.nop(), "sRS")
            c.wait("vector", "sRS", c.val("sRS"))
            for k in range(2):
                s = pair * 2 + k
                ins = c.v.tensor_scalar(t["stream"][:, m * 4 + s, :], z2[:, k, :],
                                        mean4[:, s : s + 1], rstd4[:, s : s + 1],
                                        ALU.subtract, ALU.mult)
                c.inc(ins, "vZ")
        epi_done.append(c.val("vZ"))
        ps_rel.append(c.val("vPS"))


def emit_stream_out(c, t):
    a = Arena(t["arena"], t["arena_bytes"])
    stg = a.alloc([2, 1024], F32)
    do0 = c.val("dO")
    for ti in range(NT):
        b = ti % 2
        if ti >= 2:
            c.wait("vector", "dO", do0 + 16 * (ti - 1))
        c.wait("vector", "vZ", c.val("vZ"))
        ins = c.v.tensor_copy(stg[:, b, :], t["stream"][:, ti, :])
        c.inc(ins, "vZ")
        c.wait("scalar", "vZ", c.val("vZ"))
        ins = c.s.dma_start(out=t["out"][ti * 128 : (ti + 1) * 128, :], in_=stg[:, b, :])
        c.inc(ins, "dO", 16)



def emit_p2(c, t):
    a = Arena(t["arena"], t["arena_bytes"])
    x_fm = a.alloc([8, 1024], BF16)
    xs_fm = a.alloc([8, 1024], BF16)
    y_acc = a.alloc([8, 1024], F32)
    gate_p = a.alloc([2, 8, 512], BF16)
    up_p = a.alloc([2, 8, 512], BF16)
    down_p = a.alloc([2, 4, 1024], BF16)
    act_q = a.alloc([2, 4, 512], BF16)
    sg = a.alloc([512], F32)
    gw = a.alloc([8, 8], BF16)
    lg8 = a.alloc([8, 8], F32)
    rm8 = a.alloc([8], F32)
    sc8 = a.alloc([8], F32)
    masks8 = a.alloc([8, 8], F32)
    xst = a.alloc([1024], BF16)
    zsq = a.alloc([1024], F32)
    st16 = a.alloc([8, 8], F32)
    assert a.off <= t["arena_bytes"], a.off

    s1c = st16[:, 0, :]
    s2c = st16[:, 1, :]
    mean8 = st16[:, 2, :]
    msq8 = st16[:, 3, :]
    var8 = st16[:, 4, :]
    lnv8 = st16[:, 5, :]
    rstd8 = st16[:, 6, :]

    pb = t["pb"]
    pbt = [pb[2 + i].bitcast(BF16).rearrange("p (a b) -> p a b", a=8) for i in range(2)]

    p1_tpv = c.val("tPV")
    p1_vz = c.val("vZ")

    c.wait("gpsimd", "tPV", p1_tpv)
    c.wait("gpsimd", "vZ", p1_vz)
    ins = c.g.dma_start(out=gw[:], in_=t["gate_w"].rearrange("(fc p) e -> p fc e", p=128))
    c.inc(ins, "dW", 16)
    gw_dw = c.val("dW")

    piece_used = []

    for qq in range(4):
        tiles = [qq * 8 + su for su in range(8)]
        c.wait("tensor", "vZ", p1_vz)

        for su in range(8):
            ti = tiles[su]
            gidx = qq * 16 + su
            bk = gidx % 2
            if qq + su > 0:
                c.wait("tensor", "vXF", c.val("vXF"))
            ins = None
            for fc in range(8):
                ins = c.te.transpose(pbt[bk][:, fc, :], t["stream"][:, ti, fc * 128 : (fc + 1) * 128], t["ident"][:])
            c.inc(ins, "tXF")
            c.wait("vector", "tXF", c.val("tXF"))
            c.v.nop(cycle_cnt=2000)
            ins = c.v.tensor_copy(x_fm[:, :, su * 128 : (su + 1) * 128], pbt[bk][:, :, :])
            c.inc(ins, "vXF")
            gbank = 6 + (su % 2)
            c.wait("tensor", "vXF", c.val("vXF"))
            if su >= 2:
                c.wait("tensor", "sG", c.val("sG") - 1)
            if qq == 0 and su == 0:
                c.wait("tensor", "dW", gw_dw)
            ins = None
            for fc in range(8):
                ins = c.te.matmul(pb[gbank][:, 0:8], x_fm[:, fc, su * 128 : (su + 1) * 128], gw[:, fc, :],
                                  start=(fc == 0), stop=(fc == 7))
            c.inc(ins, "tG")
            c.wait("scalar", "tG", c.val("tG"))
            ins = c.s.copy(lg8[:, su, :], pb[gbank][:, 0:8])
            c.inc(ins, "sG")
        c.wait("vector", "sG", c.val("sG"))
        for su in range(8):
            c.v.reduce_max(rm8[:, su : su + 1], lg8[:, su, :], axis=AX.X)
        c.v.drain()
        for su in range(8):
            c.v.tensor_scalar(masks8[:, su, :], lg8[:, su, :], rm8[:, su : su + 1], None, ALU.is_equal)
        c.v.drain()
        c.inc(c.v.nop(), "vMK")
        c.wait("scalar", "vMK", c.val("vMK"))
        ins = c.s.activation(sc8[:], rm8[:], AF.Sigmoid)
        c.s.drain()
        c.inc(c.s.nop(), "sG")

        c.wait("vector", "sG", c.val("sG"))
        for su in range(8):
            ti = tiles[su]
            ins = c.v.tensor_scalar(xst[:], t["stream"][:, ti, :], sc8[:, su : su + 1], None, ALU.mult)
            c.inc(ins, "vXS")
            gidx = su
            bk = su % 2
            c.wait("tensor", "vXS", c.val("vXS"))
            if su >= 1:
                c.wait("tensor", "vXF", c.val("vXF"))
            ins = None
            for fc in range(8):
                ins = c.te.transpose(pbt[bk][:, fc, :], xst[:, fc * 128 : (fc + 1) * 128], t["ident"][:])
            c.inc(ins, "tXF")
            c.wait("vector", "tXF", c.val("tXF"))
            ins = c.v.tensor_copy(xs_fm[:, :, su * 128 : (su + 1) * 128], pbt[bk][:, :, :])
            c.inc(ins, "vXF")

        c.v.memset(y_acc[:], 0.0)

        for e in range(9):
            for iq in range(4):
                pidx = len(piece_used)
                pbuf = pidx % 2
                if pidx >= 2:
                    c.wait("gpsimd", "tDN", piece_used[pidx - 2])
                if e < 8:
                    gsrc = t["rt_gate"][e, :, iq * 512 : (iq + 1) * 512]
                    usrc = t["rt_up"][e, :, iq * 512 : (iq + 1) * 512]
                    dsrc = t["rt_down"][e, iq * 512 : (iq + 1) * 512, :]
                else:
                    gsrc = t["sh_gate"][:, iq * 512 : (iq + 1) * 512]
                    usrc = t["sh_up"][:, iq * 512 : (iq + 1) * 512]
                    dsrc = t["sh_down"][iq * 512 : (iq + 1) * 512, :]
                c.inc(c.g.dma_start(out=gate_p[:, pbuf], in_=gsrc.rearrange("(fc p) i -> p fc i", p=128)), "dW", 16)
                c.inc(c.g.dma_start(out=up_p[:, pbuf], in_=usrc.rearrange("(fc p) i -> p fc i", p=128)), "dW", 16)
                c.inc(c.g.dma_start(out=down_p[:, pbuf], in_=dsrc.rearrange("(ic p) h -> p ic h", p=128)), "dW", 16)
                dw_here = c.val("dW")
                xin = x_fm if e == 8 else xs_fm
                for tt in range(2):
                    for ic in range(4):
                        gk = (ic % 2) * 2
                        uk = gk + 1
                        c.wait("tensor", "dW", dw_here - 16)
                        if True:
                            c.wait("tensor", "vAC", c.val("vAC") - 1 if (c.val("vAC") >= 2) else 0)
                        ins = None
                        for fc in range(8):
                            ins = c.te.matmul(pb[gk], gate_p[:, pbuf, fc, ic * 128 : (ic + 1) * 128],
                                              xin[:, fc, tt * 512 : (tt + 1) * 512],
                                              start=(fc == 0), stop=(fc == 7))
                        c.inc(ins, "tGU")
                        ins = None
                        for fc in range(8):
                            ins = c.te.matmul(pb[uk], up_p[:, pbuf, fc, ic * 128 : (ic + 1) * 128],
                                              xin[:, fc, tt * 512 : (tt + 1) * 512],
                                              start=(fc == 0), stop=(fc == 7))
                        c.inc(ins, "tGU")
                        c.wait("scalar", "tGU", c.val("tGU") - 1)
                        c.wait("scalar", "vAC", c.val("vAC"))
                        ins = c.s.activation(sg[:], pb[gk], AF.Silu)
                        c.inc(ins, "sSI")
                        c.wait("vector", "tGU", c.val("tGU"))
                        c.wait("vector", "sSI", c.val("sSI"))
                        ins = c.v.tensor_tensor(act_q[:, tt, ic, :], sg[:], pb[uk], ALU.mult)
                        c.inc(ins, "vAC")
                    for s4 in range(4):
                        su = tt * 4 + s4
                        dbank = 4 if (su % 2 == 0) else 6
                        c.wait("tensor", "dW", dw_here)
                        c.wait("tensor", "vAC", c.val("vAC"))
                        if c.val("tDN") >= 2:
                            c.wait("tensor", "vYA", c.val("vYA") - 1)
                        ins = None
                        for ic in range(4):
                            for nh in range(2):
                                ins = c.te.matmul(pb[dbank + nh], act_q[:, tt, ic, s4 * 128 : (s4 + 1) * 128],
                                                  down_p[:, pbuf, ic, nh * 512 : (nh + 1) * 512],
                                                  start=(ic == 0), stop=(ic == 3))
                        c.inc(ins, "tDN")
                        c.wait("vector", "tDN", c.val("tDN"))
                        psflat = t["pb_flat"][:, dbank * 512 : (dbank + 2) * 512]
                        if e < 8:
                            ins = c.v.scalar_tensor_tensor(y_acc[:, su, :], psflat, masks8[:, su, e : e + 1],
                                                           y_acc[:, su, :], ALU.mult, ALU.add)
                        else:
                            ins = c.v.scalar_tensor_tensor(y_acc[:, su, :], psflat, 1.0,
                                                           y_acc[:, su, :], ALU.mult, ALU.add)
                        c.inc(ins, "vYA")
                piece_used.append(c.val("tDN"))

        for su in range(8):
            ti = tiles[su]
            ins = c.v.tensor_tensor(y_acc[:, su, :], t["stream"][:, ti, :], y_acc[:, su, :], ALU.add)
        c.v.drain()
        for su in range(8):
            c.v.reduce_sum(s1c[:, su : su + 1], y_acc[:, su, :], axis=AX.X)
            c.v.tensor_tensor(zsq[:], y_acc[:, su, :], y_acc[:, su, :], ALU.mult)
            c.v.drain()
            c.v.reduce_sum(s2c[:, su : su + 1], zsq[:], axis=AX.X)
        c.v.drain()
        c.v.tensor_scalar(mean8[:], s1c[:], 1.0 / 1024.0, None, ALU.mult)
        c.v.drain()
        c.v.tensor_tensor(msq8[:], mean8[:], mean8[:], ALU.mult)
        c.v.drain()
        c.v.scalar_tensor_tensor(var8[:], s2c[:], 1.0 / 1024.0, msq8[:], ALU.mult, ALU.subtract)
        c.v.drain()
        c.v.tensor_scalar(var8[:], var8[:], EPS, None, ALU.add)
        c.v.drain()
        c.inc(c.v.nop(), "vST")
        c.wait("scalar", "vST", c.val("vST"))
        c.s.activation(lnv8[:], var8[:], AF.Ln)
        c.s.drain()
        c.s.activation(rstd8[:], lnv8[:], AF.Exp, scale=-0.5)
        c.s.drain()
        c.inc(c.s.nop(), "sRS")
        c.wait("vector", "sRS", c.val("sRS"))
        for su in range(8):
            ti = tiles[su]
            ins = c.v.tensor_scalar(t["stream"][:, ti, :], y_acc[:, su, :],
                                    mean8[:, su : su + 1], rstd8[:, su : su + 1],
                                    ALU.subtract, ALU.mult)
            c.inc(ins, "vZ")



def emit_p3(c, t):
    a = Arena(t["arena"], t["arena_bytes"])
    wqkv = a.alloc([8, 3072], BF16)
    wout = a.alloc([8, 1024], BF16)
    x_fm = a.alloc([8, 256], BF16)
    raw = a.alloc([2, 1024], F32)
    q_tm = a.alloc([2, 1024], BF16)
    k_tm = a.alloc([2, 1024], BF16)
    q64 = a.alloc([16, 256], BF16)
    k64 = a.alloc([16, 256], BF16)
    v_ext = a.alloc([2, 16, 65], BF16)
    atb = a.alloc([2, 2, 256], BF16)
    num_sb = a.alloc([2, 16, 65], F32)
    ov = a.alloc([2, 1024], BF16)
    o_fm = a.alloc([8, 256], BF16)
    s_sb = a.alloc([16, 65], F32)
    s_bf = a.alloc([16, 65], BF16)
    m0 = a.alloc([256], F32)
    m1 = a.alloc([256], F32)
    _zoff = (a.off + 3) // 4 * 4
    z1k = a.alloc([1024], F32)
    zsq = a.alloc([1024], F32)
    elu_t2 = _carve(t["arena"], _zoff, [2, 1024], F32)
    st4 = a.alloc([8, 2], F32)
    assert a.off <= t["arena_bytes"], a.off

    s1c = st4[:, 0, :]
    s2c = st4[:, 1, :]
    mean2 = st4[:, 2, :]
    msq2 = st4[:, 3, :]
    var2 = st4[:, 4, :]
    lnv2 = st4[:, 5, :]
    rstd2 = st4[:, 6, :]

    pb = t["pb"]
    pbt = [pb[i].bitcast(BF16).rearrange("p (a b) -> p a b", a=8) for i in range(2)]
    pp = [t["pb_flat"][:, 2 * 512 : 4 * 512], t["pb_flat"][:, 4 * 512 : 6 * 512]]
    p6 = pb[6].rearrange("p (a b) -> p a b", a=2)
    p7 = pb[7][:, 0 : 130].rearrange("p (a b) -> p a b", a=2)

    p2_vz = c.val("vZ")

    c.wait("gpsimd", "vZ", p2_vz)
    c.inc(c.g.dma_start(out=wqkv[:], in_=t["wqkvT"].rearrange("(fc p) o -> p fc o", p=128)), "dW", 16)
    c.inc(c.g.dma_start(out=wout[:], in_=t["woutT"].rearrange("(fc p) o -> p fc o", p=128)), "dW", 16)
    dw_p3 = c.val("dW")
    c.g.memset(m0[:], 1.0)
    ins = c.g.affine_select(out=m0[:], in_=m0[:], compare_op=ALU.is_ge, fill=0.0,
                            base=0, pattern=[[1, 256]], channel_multiplier=-1)
    c.inc(ins, "gI")
    c.g.memset(m1[:], 1.0)
    ins = c.g.affine_select(out=m1[:], in_=m1[:], compare_op=ALU.is_ge, fill=0.0,
                            base=-128, pattern=[[1, 256]], channel_multiplier=-1)
    c.inc(ins, "gI")
    gi_p3 = c.val("gI")

    conv_stage = a.alloc([2, 1024], BF16)
    convs = []
    marks = {}
    for nm_src, nm_dst, nfc in (("w1T", "w1c", 8), ("w2T", "w2c", 8), ("h1T", "h1c", 32), ("h2T", "h2c", 32)):
        for fc in range(nfc):
            for ob in range(4):
                convs.append((t[nm_src][fc * 128 : (fc + 1) * 128, ob * 1024 : (ob + 1) * 1024],
                              t[nm_dst][:, fc, ob * 1024 : (ob + 1) * 1024]))
        marks[nm_dst] = len(convs)
    for fc in range(32):
        convs.append((t["w3T"][fc * 128 : (fc + 1) * 128, :], t["w3c"][:, fc, :]))
    marks["w3c"] = len(convs)
    conv_marks = {}
    for i, (srcap, dstap) in enumerate(convs):
        n = srcap.shape[-1]
        if i >= 2:
            c.wait("gpsimd", "dCS", 16 * (i - 1))
        ins = c.g.dma_start(out=conv_stage[:, i % 2, 0 : n], in_=srcap)
        c.inc(ins, "dC", 16)
        c.wait("sync", "dC", 16 * (i + 1))
        ins = c.sy.dma_start(out=dstap, in_=conv_stage[:, i % 2, 0 : n])
        c.inc(ins, "dCS", 16)
        if i + 1 == marks["w2c"]:
            conv_marks["w1w2"] = c.val("dCS")
        if i + 1 == marks["h1c"]:
            conv_marks["h1"] = c.val("dCS")
        if i + 1 == marks["h2c"]:
            conv_marks["h2"] = c.val("dCS")
    conv_marks["w3"] = c.val("dCS")
    t["conv_marks"] = conv_marks

    c.v.memset(s_sb[:], 0.0)
    c.v.memset(s_bf[:], 0.0)
    c.v.memset(v_ext[:, :, :, 64 : 65], 1.0)
    c.v.drain()
    c.inc(c.v.nop(), "vSU")
    c.inc(c.v.nop(), "vEL")

    for cc in range(16):
        ti0 = 2 * cc
        for sub in range(2):
            g = cc * 2 + sub
            bk = g % 2
            if g == 0:
                c.wait("tensor", "vZ", p2_vz)
                c.wait("tensor", "vEL", c.val("vEL"))
                c.wait("tensor", "vSU", c.val("vSU"))
                c.wait("tensor", "gI", gi_p3)
            else:
                c.wait("tensor", "vXF", c.val("vXF"))
                if cc > 0:
                    c.wait("tensor", "vOT", c.val("vOT"))
            ins = None
            for fc in range(8):
                ins = c.te.transpose(pbt[bk][:, fc, :], t["stream"][:, ti0 + sub, fc * 128 : (fc + 1) * 128], t["ident"][:])
            c.inc(ins, "tXF")
            c.wait("vector", "tXF", c.val("tXF"))
            ins = c.v.tensor_copy(x_fm[:, :, sub * 128 : (sub + 1) * 128], pbt[bk][:, :, :])
            c.inc(ins, "vXF")

        c.wait("tensor", "vXF", c.val("vXF"))
        if cc == 0:
            c.wait("tensor", "dW", dw_p3)
        for which in range(3):
            base = which * 1024
            for sub in range(2):
                j = sub
                c.wait("tensor", "sQE", c.val("sQE"))
                c.wait("tensor", "vEL", c.val("vEL"))
                c.wait("tensor", "vPS", c.val("vPS"))
                ins = None
                for fc in range(8):
                    for nh in range(2):
                        ins = c.te.matmul(pp[j][:, nh * 512 : (nh + 1) * 512],
                                          x_fm[:, fc, sub * 128 : (sub + 1) * 128],
                                          wqkv[:, fc, base + nh * 512 : base + (nh + 1) * 512],
                                          start=(fc == 0), stop=(fc == 7))
                c.inc(ins, "tQKV")
                if which == 2:
                    c.wait("vector", "tQKV", c.val("tQKV"))
                    if cc > 0 and sub == 0:
                        c.wait("vector", "tSU", c.val("tSU"))
                    ins = c.v.tensor_copy(v_ext[:, sub, :, 0 : 64],
                                          pp[j].rearrange("p (h d) -> p h d", d=64))
                    c.inc(ins, "vEL")
                else:
                    c.wait("scalar", "tQKV", c.val("tQKV"))
                    if sub == 0:
                        c.wait("scalar", "vEL", c.val("vEL"))
                    ins = c.s.copy(raw[:, sub, :], pp[j])
                    c.inc(ins, "sQE")
            if which < 2:
                rawf = raw[:].rearrange("p a b -> p (a b)")
                eluf = elu_t2[:].rearrange("p a b -> p (a b)")
                c.wait("vector", "sQE", c.val("sQE"))
                ins = c.v.tensor_scalar(eluf, rawf, 0.0, None, ALU.min)
                c.inc(ins, "vEL")
                c.wait("scalar", "vEL", c.val("vEL"))
                ins = c.s.activation(eluf, eluf, AF.Exp)
                c.inc(ins, "sQE")
                if which == 0:
                    ins = c.v.tensor_scalar(rawf, rawf, 0.125, 0.0, ALU.mult, ALU.max)
                    c.v.drain()
                    c.wait("vector", "sQE", c.val("sQE"))
                    ins = c.v.scalar_tensor_tensor(q_tm[:].rearrange("p a b -> p (a b)"), eluf, 0.125, rawf, ALU.mult, ALU.add)
                    c.inc(ins, "vEL")
                else:
                    ins = c.v.tensor_scalar(rawf, rawf, 0.0, None, ALU.max)
                    c.v.drain()
                    c.wait("vector", "sQE", c.val("sQE"))
                    ins = c.v.tensor_tensor(k_tm[:].rearrange("p a b -> p (a b)"), eluf, rawf, ALU.add)
                    c.inc(ins, "vEL")

        c.wait("tensor", "vEL", c.val("vEL"))
        tgt = 0
        for src_tm, dst64 in ((q_tm, q64), (k_tm, k64)):
            for sub in range(2):
                for hg in range(2):
                    g = cc * 8 + tgt
                    tgt += 1
                    bk = g % 4
                    if c.val("vKT") >= 4:
                        c.wait("tensor", "vKT", c.val("vKT") - 3)
                    ins = None
                    for j in range(8):
                        h = hg * 8 + j
                        ins = c.te.transpose(pb[bk].bitcast(BF16)[0 : 64, j * 128 : (j + 1) * 128],
                                             src_tm[:, sub, h * 64 : (h + 1) * 64], t["ident"][:])
                    c.inc(ins, "tKT")
                    c.wait("vector", "tKT", c.val("tKT"))
                    ins = c.v.tensor_copy(dst64[0 : 64, hg * 8 : (hg + 1) * 8, sub * 128 : (sub + 1) * 128],
                                          pb[bk].bitcast(BF16)[0 : 64, :].rearrange("p (a b) -> p a b", a=8))
                    c.inc(ins, "vKT")

        c.wait("tensor", "vKT", c.val("vKT"))
        pa = [pb[i].rearrange("p (a b) -> p a b", a=2) for i in range(4)]
        pn = [pb[6][:, 0 : 130].rearrange("p (a b) -> p a b", a=2),
              pb[7][:, 0 : 130].rearrange("p (a b) -> p a b", a=2)]
        mask_cnt = [c.val("vMA")]
        tnu_base = c.val("tNU")
        def emit_attnT(h):
            ab = h % 4
            if h >= 4:
                c.wait("tensor", "vMA", mask_cnt[0] + 2 * (h - 3))
            ins = None
            for mhalf in range(2):
                ins = c.te.matmul(pa[ab][:, mhalf, :], k64[0 : 64, h, mhalf * 128 : (mhalf + 1) * 128],
                                  q64[0 : 64, h, :], start=True, stop=True)
            c.inc(ins, "tAT")
            c.wait("vector", "tAT", c.val("tAT"))
            if h >= 2:
                c.wait("vector", "tNU", tnu_base + (h - 1))
            elif cc > 0:
                c.wait("vector", "tNU", tnu_base)
            ins2 = None
            for mhalf, mk_ in ((0, m0), (1, m1)):
                ins2 = c.v.tensor_tensor(atb[:, h % 2, mhalf, :], pa[h % 4][:, mhalf, :], mk_[:], ALU.mult)
                c.inc(ins2, "vMA")
        def emit_num(h):
            c.wait("tensor", "vMA", mask_cnt[0] + 2 * (h + 1))
            if h >= 2 or cc > 0:
                c.wait("tensor", "sNU", c.val("sNU") - 1 if c.val("sNU") >= 2 else 0)
            ins = None
            for lhalf in range(2):
                sl = slice(lhalf * 128, (lhalf + 1) * 128)
                c.te.matmul(pn[h % 2][:, lhalf, :], atb[:, h % 2, 0, sl], v_ext[:, 0, h, :], start=True, stop=False)
                c.te.matmul(pn[h % 2][:, lhalf, :], atb[:, h % 2, 1, sl], v_ext[:, 1, h, :], start=False, stop=False)
                ins = c.te.matmul(pn[h % 2][:, lhalf, :], q64[0 : 64, h, sl], s_bf[0 : 64, h, :], start=False, stop=True)
            c.inc(ins, "tNU")
            c.wait("scalar", "tNU", c.val("tNU"))
            ins = c.s.copy(num_sb[:, :, h, :], pn[h % 2][:, :, :])
            c.inc(ins, "sNU")
        for h in range(16):
            emit_attnT(h)
            if h >= 2:
                emit_num(h - 2)
        emit_num(14)
        emit_num(15)

        c.wait("tensor", "vMA", c.val("vMA"))
        for hb in range(4):
            pS = pb[2 + hb][:, 0 : 260].rearrange("p (a b) -> p a b", a=4)
            ins = None
            for hh in range(4):
                h = hb * 4 + hh
                c.te.matmul(pS[0 : 64, hh, :], k_tm[:, 0, h * 64 : (h + 1) * 64], v_ext[:, 0, h, :],
                            start=True, stop=False)
                ins = c.te.matmul(pS[0 : 64, hh, :], k_tm[:, 1, h * 64 : (h + 1) * 64], v_ext[:, 1, h, :],
                                  start=False, stop=True)
            c.inc(ins, "tSU")
            c.wait("vector", "tSU", c.val("tSU"))
            ins = c.v.tensor_tensor(s_sb[0 : 64, hb * 4 : (hb + 1) * 4, :],
                                    s_sb[0 : 64, hb * 4 : (hb + 1) * 4, :], pS[0 : 64, :, :], ALU.add)
            c.v.drain()
            ins = c.v.tensor_copy(s_bf[0 : 64, hb * 4 : (hb + 1) * 4, :],
                                  s_sb[0 : 64, hb * 4 : (hb + 1) * 4, :])
            c.inc(ins, "vSU")

        c.wait("vector", "sNU", c.val("sNU"))
        ins = c.v.tensor_scalar(num_sb[:, :, :, 64 : 65], num_sb[:, :, :, 64 : 65], 1e-6, None, ALU.max)
        c.v.drain()
        ins = c.v.reciprocal(num_sb[:, :, :, 64 : 65], num_sb[:, :, :, 64 : 65])
        c.v.drain()
        rdb = num_sb[:, :, :, 64 : 65].broadcast_to([128, 2, 16, 64])
        ins = c.v.tensor_tensor(ov[:].rearrange("p a (h d) -> p a h d", d=64),
                                num_sb[:, :, :, 0 : 64], rdb, ALU.mult)
        c.inc(ins, "vDV")

        c.wait("tensor", "vDV", c.val("vDV"))
        for lhalf in range(2):
            g = cc * 2 + lhalf
            bk = g % 2
            if True:
                c.wait("tensor", "vOT", c.val("vOT") - 1 if c.val("vOT") >= 2 else 0)
            ins = None
            for fc in range(8):
                ins = c.te.transpose(pbt[bk][:, fc, :], ov[:, lhalf, fc * 128 : (fc + 1) * 128], t["ident"][:])
            c.inc(ins, "tOT")
            c.wait("vector", "tOT", c.val("tOT"))
            ins = c.v.tensor_copy(o_fm[:, :, lhalf * 128 : (lhalf + 1) * 128], pbt[bk][:, :, :])
            c.inc(ins, "vOT")

        c.wait("tensor", "vOT", c.val("vOT"))
        c.wait("tensor", "vSU", c.val("vSU"))
        for sub in range(2):
            j = sub
            ins = None
            for fc in range(8):
                for nh in range(2):
                    ins = c.te.matmul(pp[j][:, nh * 512 : (nh + 1) * 512],
                                      o_fm[:, fc, sub * 128 : (sub + 1) * 128],
                                      wout[:, fc, nh * 512 : (nh + 1) * 512],
                                      start=(fc == 0), stop=(fc == 7))
            c.inc(ins, "tWO")
        c.wait("vector", "tWO", c.val("tWO"))
        for sub in range(2):
            ins = c.v.tensor_tensor(z1k[:], t["stream"][:, ti0 + sub, :], pp[sub], ALU.add)
            c.inc(ins, "vPS")
            c.v.drain()
            c.v.reduce_sum(s1c[:, sub : sub + 1], z1k[:], axis=AX.X)
            c.v.tensor_tensor(zsq[:], z1k[:], z1k[:], ALU.mult)
            c.v.drain()
            c.v.reduce_sum(s2c[:, sub : sub + 1], zsq[:], axis=AX.X)
            c.v.drain()
            c.v.tensor_scalar(mean2[:, sub : sub + 1], s1c[:, sub : sub + 1], 1.0 / 1024.0, None, ALU.mult)
            c.v.drain()
            c.v.tensor_tensor(msq2[:, sub : sub + 1], mean2[:, sub : sub + 1], mean2[:, sub : sub + 1], ALU.mult)
            c.v.drain()
            c.v.scalar_tensor_tensor(var2[:, sub : sub + 1], s2c[:, sub : sub + 1], 1.0 / 1024.0,
                                     msq2[:, sub : sub + 1], ALU.mult, ALU.subtract)
            c.v.drain()
            c.v.tensor_scalar(var2[:, sub : sub + 1], var2[:, sub : sub + 1], EPS, None, ALU.add)
            c.v.drain()
            c.inc(c.v.nop(), "vST")
            c.wait("scalar", "vST", c.val("vST"))
            c.s.activation(lnv2[:, sub : sub + 1], var2[:, sub : sub + 1], AF.Ln)
            c.s.drain()
            c.s.activation(rstd2[:, sub : sub + 1], lnv2[:, sub : sub + 1], AF.Exp, scale=-0.5)
            c.s.drain()
            c.inc(c.s.nop(), "sRS")
            c.wait("vector", "sRS", c.val("sRS"))
            ins = c.v.tensor_scalar(t["stream"][:, ti0 + sub, :], z1k[:],
                                    mean2[:, sub : sub + 1], rstd2[:, sub : sub + 1],
                                    ALU.subtract, ALU.mult)
            c.inc(ins, "vZ")



def emit_p4(c, t):
    a = Arena(t["arena"], t["arena_bytes"])
    off0 = a.off
    x_fm = a.alloc([8, 512], BF16)
    w1p = a.alloc([2, 8, 256], BF16)
    w2p = a.alloc([2, 8, 256], BF16)
    sgl = a.alloc([512], F32)
    h1p = a.alloc([2, 32, 128], BF16)
    w3p = _carve(t["arena"], off0, [2, 32, 256], BF16)
    h_fm = a.alloc([32, 512], BF16)
    hh_fm = a.alloc([32, 512], BF16)
    scol = a.alloc([4, 32], F32)
    s3b = a.alloc([1024], F32)
    z2 = a.alloc([2, 1024], F32)
    zsq = a.alloc([1024], F32)
    st4 = a.alloc([8, 4], F32)
    assert a.off <= t["arena_bytes"], a.off
    s3r = z2[0:1, 0, :]
    onec = z2[0:1, 1, 0:128]

    s1c = st4[:, 0, :]
    s2c = st4[:, 1, :]
    mean4 = st4[:, 2, :]
    msq4 = st4[:, 3, :]
    var4 = st4[:, 4, :]
    lnv4 = st4[:, 5, :]
    rstd4 = st4[:, 6, :]

    pb = t["pb"]
    pbt = [pb[i].bitcast(BF16).rearrange("p (a b) -> p a b", a=8) for i in range(2)]
    cm = t["conv_marks"]
    p3_vz = c.val("vZ")

    c.wait("sync", "vZ", p3_vz)
    for i, nm in enumerate(("s1c", "s2c", "sh1c", "sh2c")):
        c.inc(c.sy.dma_start(out=scol[:, i, :], in_=t[nm]), "dX", 16)
    c.inc(c.sy.dma_start(out=s3r, in_=t["s3row"]), "dX", 16)
    dx_p4 = c.val("dX")
    c.wait("vector", "dX", dx_p4)
    c.v.memset(onec, 1.0)
    c.v.drain()
    c.inc(c.v.nop(), "vH")
    c.wait("tensor", "vZ", p3_vz)
    c.wait("tensor", "vH", c.val("vH"))
    c.wait("tensor", "dX", dx_p4)
    ins = None
    for nh in range(2):
        ins = c.te.matmul(t["pb_flat"][:, (6 + nh) * 512 : (7 + nh) * 512], onec,
                          s3r[:, nh * 512 : (nh + 1) * 512], start=True, stop=True)
    c.inc(ins, "tH")
    c.wait("vector", "tH", c.val("tH"))
    ins = c.v.tensor_copy(s3b[:], t["pb_flat"][:, 6 * 512 : 8 * 512])
    c.v.drain()
    c.inc(ins, "vH")
    c.inc(c.v.nop(), "vPS")

    wu = []
    hpieces = []
    w3u = []
    out_dmas = []

    for tt in range(8):
        ti0 = tt * 4
        for g4 in range(4):
            bk = g4 % 2
            c.wait("tensor", "vH", c.val("vH"))
            if tt > 0 and g4 == 0:
                c.wait("tensor", "vPS", c.val("vPS") - 3)
            ins = None
            for fc in range(8):
                ins = c.te.transpose(pbt[bk][:, fc, :], t["stream"][:, ti0 + g4, fc * 128 : (fc + 1) * 128], t["ident"][:])
            c.inc(ins, "tH")
            c.wait("vector", "tH", c.val("tH"))
            ins = c.v.tensor_copy(x_fm[:, :, g4 * 128 : (g4 + 1) * 128], pbt[bk][:, :, :])
            c.inc(ins, "vH")

        for ob in range(16):
            pidx = len(wu)
            buf = pidx % 2
            if pidx % 16 < 2:
                c.wait("gpsimd", "tH", c.val("tH"))
            elif pidx >= 2:
                c.wait("gpsimd", "tH", wu[pidx - 2])
            if pidx == 0:
                c.wait("gpsimd", "dCS", cm["w1w2"])
            c.inc(c.g.dma_start(out=w1p[:, buf], in_=t["w1c"][:, :, ob * 256 : (ob + 1) * 256]), "dWH", 16)
            c.inc(c.g.dma_start(out=w2p[:, buf], in_=t["w2c"][:, :, ob * 256 : (ob + 1) * 256]), "dWH", 16)
            c.wait("tensor", "dWH", c.val("dWH"))
            if ob == 0:
                c.wait("tensor", "vH", c.val("vH"))
                c.wait("tensor", "vPS", c.val("vPS") - 2 if tt > 0 else 0)
            for oc in range(2):
                og = ob * 2 + oc
                gk = (og % 3) * 2
                uk = gk + 1
                if c.val("vH") >= 3:
                    c.wait("tensor", "vH", c.val("vH") - 2)
                ins = None
                for fc in range(8):
                    ins = c.te.matmul(pb[gk], w1p[:, buf, fc, oc * 128 : (oc + 1) * 128], x_fm[:, fc, :],
                                      start=(fc == 0), stop=(fc == 7))
                c.inc(ins, "tH")
                ins = None
                for fc in range(8):
                    ins = c.te.matmul(pb[uk], w2p[:, buf, fc, oc * 128 : (oc + 1) * 128], x_fm[:, fc, :],
                                      start=(fc == 0), stop=(fc == 7))
                c.inc(ins, "tH")
                c.wait("scalar", "tH", c.val("tH") - 1)
                c.wait("scalar", "vH", c.val("vH"))
                ins = c.s.activation(sgl[:], pb[gk], AF.Silu, scale=scol[:, 0, og : og + 1])
                c.inc(ins, "sH")
                c.wait("vector", "tH", c.val("tH"))
                c.wait("vector", "sH", c.val("sH"))
                ins = c.v.scalar_tensor_tensor(h_fm[:, og, :], pb[uk], scol[:, 1, og : og + 1], sgl[:],
                                               ALU.mult, ALU.mult)
                c.inc(ins, "vH")

        for phase in range(2):
            srcbuf = h_fm if phase == 0 else hh_fm
            dstbuf = hh_fm if phase == 0 else h_fm
            cache = t["h1c"] if phase == 0 else t["h2c"]
            scidx = 2 + phase
            for ob in range(32):
                pidx = len(hpieces)
                buf = pidx % 2
                if pidx >= 2:
                    c.wait("sync", "tH", hpieces[pidx - 2])
                if tt == 0 and phase == 0 and ob == 0:
                    c.wait("sync", "dCS", cm["h1"])
                if tt == 0 and phase == 1 and ob == 0:
                    c.wait("sync", "dCS", cm["h2"])
                c.inc(c.sy.dma_start(out=h1p[:, buf], in_=cache[:, :, ob * 128 : (ob + 1) * 128]), "dK", 16)
                c.wait("tensor", "dK", c.val("dK"))
                if ob == 0:
                    c.wait("tensor", "vH", c.val("vH"))
                    c.wait("tensor", "sH", c.val("sH"))
                og = ob
                bk45 = 2 + (og % 4)
                if c.val("sH") >= 4:
                    c.wait("tensor", "sH", c.val("sH") - 3)
                ins = None
                for ic in range(32):
                    ins = c.te.matmul(pb[bk45], h1p[:, buf, ic, :], srcbuf[:, ic, :],
                                      start=(ic == 0), stop=(ic == 31))
                c.inc(ins, "tH")
                hpieces.append(c.val("tH"))
                c.wait("scalar", "tH", c.val("tH"))
                ins = c.s.activation(dstbuf[:, og, :], pb[bk45], AF.Gelu, scale=scol[:, scidx, og : og + 1])
                c.inc(ins, "sH")

        for qtr in range(4):
            pidx = len(w3u)
            buf = pidx % 2
            if pidx % 4 < 2:
                c.wait("sync", "tH", c.val("tH"))
            elif pidx >= 2:
                c.wait("sync", "tH", w3u[pidx - 2])
            if tt == 0 and qtr == 0:
                c.wait("sync", "dCS", cm["w3"])
            c.inc(c.sy.dma_start(out=w3p[:, buf], in_=t["w3c"][:, :, qtr * 256 : (qtr + 1) * 256]), "dK", 16)
            c.wait("tensor", "dK", c.val("dK"))
            if qtr == 0:
                c.wait("tensor", "sH", c.val("sH"))
                c.wait("tensor", "vPS", c.val("vPS"))
            for sub in range(4):
                ps_q = t["pb_flat"][:, (2 * sub) * 512 + qtr * 256 : (2 * sub) * 512 + (qtr + 1) * 256]
                ins = None
                for ic in range(32):
                    ins = c.te.matmul(ps_q, h_fm[:, ic, sub * 128 : (sub + 1) * 128],
                                      w3p[:, buf, ic, :], start=(ic == 0), stop=(ic == 31))
                c.inc(ins, "tH")
            w3u.append(c.val("tH"))
        for pair in range(2):
            for k in range(2):
                sub = pair * 2 + k
                ti = ti0 + sub
                psflat = t["pb_flat"][:, (2 * sub) * 512 : (2 * sub + 2) * 512]
                c.wait("vector", "tH", c.val("tH"))
                if len(out_dmas) >= 2:
                    c.wait("vector", "dO", out_dmas[-2])
                ins = c.v.tensor_tensor(z2[:, k, :], psflat, s3b[:], ALU.mult)
                c.inc(ins, "vPS")
                c.v.drain()
                ins = c.v.tensor_tensor(z2[:, k, :], z2[:, k, :], t["stream"][:, ti, :], ALU.add)
                c.v.drain()
                c.v.reduce_sum(s1c[:, sub : sub + 1], z2[:, k, :], axis=AX.X)
                c.v.tensor_tensor(zsq[:], z2[:, k, :], z2[:, k, :], ALU.mult)
                c.v.drain()
                c.v.reduce_sum(s2c[:, sub : sub + 1], zsq[:], axis=AX.X)
            c.v.drain()
            sl = slice(pair * 2, pair * 2 + 2)
            c.v.tensor_scalar(mean4[:, sl], s1c[:, sl], 1.0 / 1024.0, None, ALU.mult)
            c.v.drain()
            c.v.tensor_tensor(msq4[:, sl], mean4[:, sl], mean4[:, sl], ALU.mult)
            c.v.drain()
            c.v.scalar_tensor_tensor(var4[:, sl], s2c[:, sl], 1.0 / 1024.0, msq4[:, sl], ALU.mult, ALU.subtract)
            c.v.drain()
            c.v.tensor_scalar(var4[:, sl], var4[:, sl], EPS, None, ALU.add)
            c.v.drain()
            c.inc(c.v.nop(), "vST")
            c.wait("scalar", "vST", c.val("vST"))
            c.s.activation(lnv4[:, sl], var4[:, sl], AF.Ln)
            c.s.drain()
            c.s.activation(rstd4[:, sl], lnv4[:, sl], AF.Exp, scale=-0.5)
            c.s.drain()
            c.inc(c.s.nop(), "sRS")
            c.wait("vector", "sRS", c.val("sRS"))
            for k in range(2):
                sub = pair * 2 + k
                ti = ti0 + sub
                ins = c.v.tensor_scalar(z2[:, k, :], z2[:, k, :],
                                        mean4[:, sub : sub + 1], rstd4[:, sub : sub + 1],
                                        ALU.subtract, ALU.mult)
                c.v.drain()
                c.inc(c.v.nop(), "vZ")
                c.wait("scalar", "vZ", c.val("vZ"))
                ins = c.s.dma_start(out=t["out"][ti * 128 : (ti + 1) * 128, :], in_=z2[:, k, :])
                out_dmas.append(c.inc(ins, "dO", 16))


ARENA_BYTES = 146944


def build_nc(upto=4):
    nc = bass.Bass("TRN2", target_bir_lowering=False)
    t = {}
    t["arena_bytes"] = ARENA_BYTES

    t["x"] = nc.dram_tensor("x", [T, D], F32, kind="ExternalInput").ap()
    t["wqT"] = nc.dram_tensor("wqT", [D, D], F32, kind="ExternalInput").ap()
    t["wkT"] = nc.dram_tensor("wkT", [D, D], F32, kind="ExternalInput").ap()
    t["wvT"] = nc.dram_tensor("wvT", [D, D], F32, kind="ExternalInput").ap()
    t["out"] = nc.dram_tensor("out", [T, D], F32, kind="ExternalOutput").ap()
    t["gate_w"] = nc.dram_tensor("gate_w", [D, 8], F32, kind="ExternalInput").ap()
    t["sh_gate"] = nc.dram_tensor("sh_gate", [D, 2048], F32, kind="ExternalInput").ap()
    t["sh_up"] = nc.dram_tensor("sh_up", [D, 2048], F32, kind="ExternalInput").ap()
    t["sh_down"] = nc.dram_tensor("sh_down", [2048, D], F32, kind="ExternalInput").ap()
    t["rt_gate"] = nc.dram_tensor("rt_gate", [8, D, 2048], F32, kind="ExternalInput").ap()
    t["rt_up"] = nc.dram_tensor("rt_up", [8, D, 2048], F32, kind="ExternalInput").ap()
    t["rt_down"] = nc.dram_tensor("rt_down", [8, 2048, D], F32, kind="ExternalInput").ap()
    t["wqkvT"] = nc.dram_tensor("wqkvT", [D, 3072], F32, kind="ExternalInput").ap()
    t["w1T"] = nc.dram_tensor("w1T", [D, 4096], F32, kind="ExternalInput").ap()
    t["w2T"] = nc.dram_tensor("w2T", [D, 4096], F32, kind="ExternalInput").ap()
    t["h1T"] = nc.dram_tensor("h1T", [4096, 4096], F32, kind="ExternalInput").ap()
    t["h2T"] = nc.dram_tensor("h2T", [4096, 4096], F32, kind="ExternalInput").ap()
    t["w3T"] = nc.dram_tensor("w3T", [4096, D], F32, kind="ExternalInput").ap()
    t["s1c"] = nc.dram_tensor("s1c", [128, 32], F32, kind="ExternalInput").ap()
    t["s2c"] = nc.dram_tensor("s2c", [128, 32], F32, kind="ExternalInput").ap()
    t["sh1c"] = nc.dram_tensor("sh1c", [128, 32], F32, kind="ExternalInput").ap()
    t["sh2c"] = nc.dram_tensor("sh2c", [128, 32], F32, kind="ExternalInput").ap()
    t["s3row"] = nc.dram_tensor("s3row", [1, D], F32, kind="ExternalInput").ap()
    t["w1c"] = nc.dram_tensor("w1c", [128, 8, 4096], BF16, kind="Internal").ap()
    t["w2c"] = nc.dram_tensor("w2c", [128, 8, 4096], BF16, kind="Internal").ap()
    t["h1c"] = nc.dram_tensor("h1c", [128, 32, 4096], BF16, kind="Internal").ap()
    t["h2c"] = nc.dram_tensor("h2c", [128, 32, 4096], BF16, kind="Internal").ap()
    t["w3c"] = nc.dram_tensor("w3c", [128, 32, 1024], BF16, kind="Internal").ap()
    t["woutT"] = nc.dram_tensor("woutT", [D, D], F32, kind="ExternalInput").ap()

    t["q_spill"] = nc.dram_tensor("q_spill", [8, 128, 8, 512], BF16, kind="Internal").ap()
    t["k_spill"] = nc.dram_tensor("k_spill", [8, 128, 8, 512], BF16, kind="Internal").ap()
    t["v_spill"] = nc.dram_tensor("v_spill", [8, 128, 4, 1024], BF16, kind="Internal").ap()

    import contextlib
    es = contextlib.ExitStack()
    with es:
        t["stream"] = es.enter_context(nc.sbuf_tensor("stream", [128, NT, 1024], BF16))
        t["ident"] = es.enter_context(nc.sbuf_tensor("ident", [128, 128], BF16))
        arena = es.enter_context(nc.sbuf_tensor("arena", [128, ARENA_BYTES // 2], BF16))
        t["arena"] = arena
        pbs = es.enter_context(nc.psum_tensor("pball", [128, 8, 512], F32))
        t["pb"] = [pbs[:, i, :] for i in range(8)]
        t["pb_flat"] = pbs[:].rearrange("p a b -> p (a b)")
        sems = {}
        for name in SEMS:
            sems[name] = es.enter_context(nc.semaphore(name))
        block = es.enter_context(nc.Block())

        def run_pass(eng_name):
            def f(eng):
                cx = Ctx(nc, sems, eng_name, eng)
                emit_program(cx, t, upto=upto)
            return f

        block.sync(run_pass("sync"))
        block.vector(run_pass("vector"))
        block.scalar(run_pass("scalar"))
        block.tensor(run_pass("tensor"))
        block.gpsimd(run_pass("gpsimd"))
    return nc


def host_inputs_per_core(core, inp):
    asc = np.ascontiguousarray
    return {
        "x": asc(inp["L0_attn_x"][core]),
        "wqT": asc(inp["L0_attn_Wq"].T),
        "wkT": asc(inp["L0_attn_Wk"].T),
        "wvT": asc(inp["L0_attn_Wv"].T),
        "gate_w": asc(inp["L0_ffn_gate_w"]),
        "sh_gate": asc(inp["L0_ffn_sh_gate"]),
        "sh_up": asc(inp["L0_ffn_sh_up"]),
        "sh_down": asc(inp["L0_ffn_sh_down"]),
        "rt_gate": asc(inp["L0_ffn_rt_gate"]),
        "rt_up": asc(inp["L0_ffn_rt_up"]),
        "rt_down": asc(inp["L0_ffn_rt_down"]),
        "wqkvT": asc(inp["L1_attn_W_qkv"].T),
        "woutT": asc(inp["L1_attn_W_out"].T),
        "w1T": asc(inp["L1_ffn_w1_w"].T),
        "w2T": asc(inp["L1_ffn_w2_w"].T),
        "h1T": asc(inp["L1_ffn_h1_w"].T),
        "h2T": asc(inp["L1_ffn_h2_w"].T),
        "w3T": asc(inp["L1_ffn_w3_w"].T),
        "s1c": asc(inp["L1_ffn_w1_s"].reshape(32, 128).T),
        "s2c": asc(inp["L1_ffn_w2_s"].reshape(32, 128).T),
        "sh1c": asc(inp["L1_ffn_h1_s"].reshape(32, 128).T),
        "sh2c": asc(inp["L1_ffn_h2_s"].reshape(32, 128).T),
        "s3row": asc(inp["L1_ffn_w3_s"].reshape(1, 1024)),
    }


def kernel(**inputs):
    inputs = {k: np.asarray(v) for k, v in inputs.items()}
    nc = build_nc(upto=4)
    in_maps = [host_inputs_per_core(cc, inputs) for cc in range(8)]
    res = run_bass_kernel_spmd(nc, in_maps, core_ids=list(range(8)))
    return np.stack([res.results[cc]["out"] for cc in range(8)], axis=0)
